# version 02
# baseline (speedup 1.0000x reference)
import numpy as np
import ml_dtypes
import concourse.bass as bass
import concourse.mybir as mybir
from concourse.bass import IndirectOffsetOnAxis
from concourse.bass_utils import run_bass_kernel_spmd

F32 = mybir.dt.float32
BF16 = mybir.dt.bfloat16
U32 = mybir.dt.uint32
AF = mybir.ActivationFunctionType
ALU = mybir.AluOpType
AX = mybir.AxisListType
BF16NP = ml_dtypes.bfloat16

CORES = 8
D = 1024
L = 8
NSEQ = 4096
NLOC = NSEQ // CORES
TOK = L * NLOC
TILES = TOK // 128
C4 = 4
HA = 8
HM = 4
E = 8
FF = 4096
CAP = 4096
CPC = 1024
EPS = 1e-5
OOB = float(1 << 24)


class Emit:
    def __init__(self):
        self.cnt = {}

    def mark(self, instr, sem, k=1):
        instr.then_inc(sem, k)
        self.cnt[sem.name] = self.cnt.get(sem.name, 0) + k
        return self.cnt[sem.name]

    def n(self, sem):
        return self.cnt.get(sem.name, 0)


class Scope:

    def __init__(self, nc):
        self.nc = nc
        self.cms = []

    def T(self, name, shape, dt):
        cm = self.nc.sbuf_tensor(name, shape, dt)
        t = cm.__enter__()
        self.cms.append(cm)
        return t

    def close(self):
        for cm in reversed(self.cms):
            cm.__exit__(None, None, None)
        self.cms = []


def barrier(block, em, sems_waits):
    def _b(e):
        for sem, val in sems_waits:
            e.wait_ge(sem, val)
    block.vector(_b)
    block.scalar(_b)
    block.tensor(_b)
    block.sync(_b)
    block.gpsimd(_b)


def _ln_chain(block, em, sDv, sSe, src, xhat_out, ln):
    def _v1(v):
        v.wait_ge(sDv, em.n(sDv))
        n0 = em.mark(v.reduce_sum(ln["sx"][:], src, axis=AX.X), sDv)
        v.wait_ge(sDv, n0)
        em.mark(v.tensor_scalar(ln["negmu"][:], ln["sx"][:], -1.0 / D, None, op0=ALU.mult), sDv)
    block.vector(_v1)

    def _s1(s):
        s.wait_ge(sDv, em.n(sDv))
        em.mark(s.activation(ln["dummy"][:], src, AF.Square, bias=ln["negmu"][:, 0:1],
                             scale=1.0, accum_out=ln["svar"][:]), sSe)
    block.scalar(_s1)

    def _v2(v):
        v.wait_ge(sSe, em.n(sSe))
        em.mark(v.tensor_scalar(ln["vep"][:], ln["svar"][:], 1.0 / D, EPS,
                                op0=ALU.mult, op1=ALU.add), sDv)
    block.vector(_v2)

    def _s2a(s):
        s.wait_ge(sDv, em.n(sDv))
        em.mark(s.activation(ln["rstd"][:], ln["vep"][:], AF.Ln, bias=0.0, scale=1.0), sSe)
    block.scalar(_s2a)

    def _v2b(v):
        v.wait_ge(sSe, em.n(sSe))
        em.mark(v.tensor_scalar(ln["rstd"][:], ln["rstd"][:], -0.5, None,
                                op0=ALU.mult), sDv)
    block.vector(_v2b)

    def _s2b(s):
        s.wait_ge(sDv, em.n(sDv))
        em.mark(s.activation(ln["rstd"][:], ln["rstd"][:], AF.Exp, bias=0.0, scale=1.0), sSe)
    block.scalar(_s2b)

    def _v3(v):
        v.wait_ge(sSe, em.n(sSe))
        em.mark(v.tensor_scalar(ln["nmr"][:], ln["negmu"][:], ln["rstd"][:, 0:1], None,
                                op0=ALU.mult), sDv)
    block.vector(_v3)

    def _s3(s):
        s.wait_ge(sDv, em.n(sDv))
        em.mark(s.activation(xhat_out, src, AF.Identity, bias=ln["nmr"][:, 0:1],
                             scale=ln["rstd"][:, 0:1]), sSe)
    block.scalar(_s3)
    return em.n(sSe)



def build_launch1():
    nc = bass.Bass("TRN2", target_bir_lowering=False)

    x_in = nc.dram_tensor("x", [TOK, D], F32, kind="ExternalInput").ap()
    wrT = nc.dram_tensor("wrT", [D, D], F32, kind="ExternalInput").ap()
    attr = nc.dram_tensor("attr", [1, 128], F32, kind="ExternalInput").ap()
    ipT = nc.dram_tensor("ipT", [D, 3 * D], F32, kind="ExternalInput").ap()
    ipb = nc.dram_tensor("ipb", [1, 3 * D], F32, kind="ExternalInput").ap()
    opT = nc.dram_tensor("opT", [D, D], F32, kind="ExternalInput").ap()
    opb = nc.dram_tensor("opb", [1, D], F32, kind="ExternalInput").ap()
    g1 = nc.dram_tensor("g1", [1, D], F32, kind="ExternalInput").ap()
    b1r = nc.dram_tensor("b1r", [1, D], F32, kind="ExternalInput").ap()
    wg = nc.dram_tensor("wg", [D, E], F32, kind="ExternalInput").ap()
    w1T = nc.dram_tensor("w1T", [FF, D], F32, kind="ExternalInput").ap()
    w2e = nc.dram_tensor("w2e", [FF, D], F32, kind="ExternalInput").ap()
    b1e = nc.dram_tensor("b1e", [128, FF // 128], F32, kind="ExternalInput").ap()
    b2e = nc.dram_tensor("b2e", [1, D], F32, kind="ExternalInput").ap()
    idf = nc.dram_tensor("idf", [128, 128], F32, kind="ExternalInput").ap()
    idb = nc.dram_tensor("idb", [128, 128], BF16, kind="ExternalInput").ap()
    onesrow_b = nc.dram_tensor("onesrow_b", [1, 128], BF16, kind="ExternalInput").ap()
    onesrow_f = nc.dram_tensor("onesrow_f", [1, 128], F32, kind="ExternalInput").ap()
    onescol_f = nc.dram_tensor("onescol_f", [128, 1], F32, kind="ExternalInput").ap()
    iotaE_in = nc.dram_tensor("iotaE", [128, E], F32, kind="ExternalInput").ap()

    o_w12 = nc.dram_tensor("o_w12", [D, D], BF16, kind="ExternalOutput").ap()
    o_b12 = nc.dram_tensor("o_b12", [1, D], F32, kind="ExternalOutput").ap()
    o_m = nc.dram_tensor("o_m", [HA, L], F32, kind="ExternalOutput").ap()
    o_z = nc.dram_tensor("o_z", [HA, L], F32, kind="ExternalOutput").ap()
    o_s = nc.dram_tensor("o_s", [1, L * D], F32, kind="ExternalOutput").ap()
    o_cnt = nc.dram_tensor("o_cnt", [1, L * E], F32, kind="ExternalOutput").ap()
    o_y1 = nc.dram_tensor("o_y1", [TOK, D], BF16, kind="ExternalOutput").ap()
    o_xn = nc.dram_tensor("o_xn", [TOK, D], BF16, kind="ExternalOutput").ap()
    o_idx = nc.dram_tensor("o_idx", [128, TILES], F32, kind="ExternalOutput").ap()
    o_gate = nc.dram_tensor("o_gate", [128, TILES], F32, kind="ExternalOutput").ap()

    from contextlib import ExitStack
    es = ExitStack()
    P = lambda name, shape, dt: es.enter_context(nc.psum_tensor(name, shape, dt))

    G = Scope(nc)
    xn_sb = G.T("xn_sb", [128, TILES * D], BF16)
    idf_sb = G.T("idf_sb", [128, 128], F32)
    idb_sb = G.T("idb_sb", [128, 128], BF16)
    onesb_sb = G.T("onesb_sb", [1, 128], BF16)
    onesf_sb = G.T("onesf_sb", [1, 128], F32)
    onescol_sb = G.T("onescol_sb", [128, 1], F32)
    iotaE_sb = G.T("iotaE_sb", [128, E], F32)
    attrB_sb = G.T("attrB_sb", [128, 128], BF16)
    wgp_sb = G.T("wgp_sb", [128, 8 * E], F32)
    cvec_sb = G.T("cvec_sb", [1, E], F32)
    g1B_sb = G.T("g1B_sb", [128, D], BF16)
    b1B_sb = G.T("b1B_sb", [128, D], BF16)
    idx_sb = G.T("idx_sb", [128, TILES], F32)
    gate_sb = G.T("gate_sb", [128, TILES], F32)
    cnt_sb = G.T("cnt_sb", [1, L * E], F32)
    bf2k = G.T("bf2k", [128, D], BF16)
    dummyDV = G.T("dummyDV", [128, 256], F32)
    ln = {k: G.T("ln_" + k, [128, 1], F32) for k in
          ("sx", "negmu", "svar", "vep", "rstd", "nmr")}
    ln["dummy"] = G.T("ln_dummy", [128, D], F32)

    psA = P("psA", [128, 512], F32)
    psB = P("psB", [128, 512], F32)
    psT = P("psT", [128, D], BF16)
    psF = P("psF", [128, D], F32)
    psS = P("psS", [1, D], F32)
    psC = P("psC", [1, 512], F32)

    sLd = es.enter_context(nc.semaphore("sLd"))
    sGw2 = es.enter_context(nc.semaphore("sGw2"))
    sGw1 = es.enter_context(nc.semaphore("sGw1"))
    sGrows = es.enter_context(nc.semaphore("sGrows"))
    sGwr = es.enter_context(nc.semaphore("sGwr"))
    sGip = es.enter_context(nc.semaphore("sGip"))
    sGop = es.enter_context(nc.semaphore("sGop"))
    sLdX = [es.enter_context(nc.semaphore(f"sLdX{i}")) for i in range(2)]
    sPe = es.enter_context(nc.semaphore("sPe"))
    sDv = es.enter_context(nc.semaphore("sDv"))
    sSe = es.enter_context(nc.semaphore("sSe"))
    sSt = es.enter_context(nc.semaphore("sSt"))
    block = es.enter_context(nc.Block())
    em = Emit()

    S0 = Scope(nc)
    w2_sb = S0.T("w2_sb", [128, 32 * D], BF16)
    w1m_sb = S0.T("w1m_sb", [128, FF], BF16)
    b1c_sb = S0.T("b1c_sb", [128, FF // 128], BF16)
    w12_sb = S0.T("w12_sb", [128, D], BF16)
    b12_sb = S0.T("b12_sb", [1, D], F32)
    b2_sb = S0.T("b2_sb", [1, D], F32)
    g1f_sb = S0.T("g1f_sb", [1, D], F32)
    b1f_sb = S0.T("b1f_sb", [1, D], F32)
    wg_sb = S0.T("wg_sb", [128, 8 * E], F32)
    col_sb = S0.T("col_sb", [128, 8], F32)

    def _ld(sync):
        for dst, src in ((idf_sb[:], idf), (idb_sb[:], idb), (onesb_sb[:], onesrow_b),
                         (onesf_sb[:], onesrow_f), (onescol_sb[:], onescol_f),
                         (iotaE_sb[:], iotaE_in), (b2_sb[:], b2e),
                         (g1f_sb[:], g1), (b1f_sb[:], b1r)):
            em.mark(sync.dma_start(out=dst, in_=src), sLd, 16)
        for k in range(8):
            em.mark(sync.dma_start(out=wg_sb[:, k * E:(k + 1) * E],
                                   in_=wg[k * 128:(k + 1) * 128, :]), sLd, 16)
    block.sync(_ld)
    LD_PRE = em.n(sLd)

    def _gw2(g):
        for k in range(32):
            em.mark(g.dma_start(out=w2_sb[:, k * D:(k + 1) * D],
                                in_=w2e[k * 128:(k + 1) * 128, :]), sGw2, 16)
        em.mark(g.dma_start(out=b1c_sb[:], in_=b1e), sGw2, 16)
    block.gpsimd(_gw2)
    G_W2 = em.n(sGw2)

    pe_after_M = []
    for M in range(8):
        def _gw1(g, M=M):
            if M >= 1:
                g.wait_ge(sPe, pe_after_M[M - 1])
            em.mark(g.dma_start(
                out=w1m_sb[:].rearrange("p (k i) -> p k i", k=32),
                in_=w1T[:, M * 128:(M + 1) * 128].rearrange("(k p) i -> p k i", p=128)),
                sGw1, 16)
        block.gpsimd(_gw1)
        g_w1m = em.n(sGw1)

        def _tmm(t, M=M, g_w1m=g_w1m):
            t.wait_ge(sGw2, G_W2)
            t.wait_ge(sGw1, g_w1m)
            if M >= 1:
                t.wait_ge(sDv, em.n(sDv))
            for K in range(32):
                t.matmul(psA[:, :], w1m_sb[:, K * 128:(K + 1) * 128],
                         w2_sb[:, K * D:K * D + 512], start=(K == 0), stop=(K == 31))
            for K in range(32):
                i = t.matmul(psB[:, :], w1m_sb[:, K * 128:(K + 1) * 128],
                             w2_sb[:, K * D + 512:(K + 1) * D],
                             start=(K == 0), stop=(K == 31))
                if K == 31 and M != 0:
                    em.mark(i, sPe)
            if M == 0:
                for K in range(32):
                    t.matmul(psS[0:1, 0:512], b1c_sb[:, K:K + 1],
                             w2_sb[:, K * D:K * D + 512],
                             start=(K == 0), stop=(K == 31), skip_group_check=True)
                for K in range(32):
                    i = t.matmul(psS[0:1, 512:1024], b1c_sb[:, K:K + 1],
                                 w2_sb[:, K * D + 512:(K + 1) * D],
                                 start=(K == 0), stop=(K == 31), skip_group_check=True)
                    if K == 31:
                        em.mark(i, sPe)
        block.tensor(_tmm)
        pe_after_M.append(em.n(sPe))

        def _vev(v, M=M):
            v.wait_ge(sPe, pe_after_M[M])
            if M >= 1:
                v.wait_ge(sSt, em.n(sSt))
            em.mark(v.tensor_copy(w12_sb[:, 0:512], psA[:, :]), sDv)
            em.mark(v.tensor_copy(w12_sb[:, 512:1024], psB[:, :]), sDv)
        block.vector(_vev)

        def _st(sync, M=M):
            sync.wait_ge(sDv, em.n(sDv))
            em.mark(sync.dma_start(out=o_w12[M * 128:(M + 1) * 128, :],
                                   in_=w12_sb[:]), sSt, 16)
        block.sync(_st)

    def _b12(v):
        v.wait_ge(sPe, pe_after_M[0])
        v.wait_ge(sLd, LD_PRE)
        em.mark(v.tensor_tensor(b12_sb[0:1, 0:512], psS[0:1, 0:512],
                                b2_sb[0:1, 0:512], op=ALU.add), sDv)
        em.mark(v.tensor_tensor(b12_sb[0:1, 512:1024], psS[0:1, 512:1024],
                                b2_sb[0:1, 512:1024], op=ALU.add), sDv)
    block.vector(_b12)

    def _b12st(sync):
        sync.wait_ge(sDv, em.n(sDv))
        em.mark(sync.dma_start(out=o_b12, in_=b12_sb[:]), sSt, 16)
    block.sync(_b12st)

    def _g1c(t):
        t.wait_ge(sLd, LD_PRE)
        t.wait_ge(sPe, pe_after_M[-1])
        for k in range(8):
            i = t.transpose(psF[:, k:k + 1], g1f_sb[0:1, k * 128:(k + 1) * 128],
                            idf_sb[0:1, 0:1])
        em.mark(i, sPe)
    block.tensor(_g1c)

    def _g1cv(v):
        v.wait_ge(sPe, em.n(sPe))
        n0 = em.mark(v.tensor_copy(col_sb[:], psF[:, 0:8]), sDv)
        v.wait_ge(sDv, n0)
        for k in range(8):
            em.mark(v.tensor_scalar(wgp_sb[:, k * E:(k + 1) * E],
                                    wg_sb[:, k * E:(k + 1) * E],
                                    col_sb[:, k:k + 1], None, op0=ALU.mult), sDv)
    block.vector(_g1cv)

    def _b1c2(t):
        t.wait_ge(sDv, em.n(sDv))
        for k in range(8):
            i = t.transpose(psF[:, 8 + k:9 + k], b1f_sb[0:1, k * 128:(k + 1) * 128],
                            idf_sb[0:1, 0:1])
        em.mark(i, sPe)
    block.tensor(_b1c2)

    def _b1cv(v):
        v.wait_ge(sPe, em.n(sPe))
        em.mark(v.tensor_copy(col_sb[:], psF[:, 8:16]), sDv)
    block.vector(_b1cv)

    def _cvec(t):
        t.wait_ge(sDv, em.n(sDv))
        for k in range(8):
            i = t.matmul(psC[0:1, 0:E], col_sb[:, k:k + 1], wg_sb[:, k * E:(k + 1) * E],
                         start=(k == 0), stop=(k == 7))
        em.mark(i, sPe)
    block.tensor(_cvec)

    def _cvecv(v):
        v.wait_ge(sPe, em.n(sPe))
        em.mark(v.tensor_copy(cvec_sb[:], psC[0:1, 0:E]), sDv)
    block.vector(_cvecv)

    def _gldg(g):
        em.mark(g.dma_start(out=g1B_sb[0:1, :], in_=g1), sGrows, 16)
        em.mark(g.dma_start(out=b1B_sb[0:1, :], in_=b1r), sGrows, 16)
        em.mark(g.dma_start(out=attrB_sb[0:1, :], in_=attr), sGrows, 16)
    block.gpsimd(_gldg)
    G_ROWS = em.n(sGrows)

    for dst in (g1B_sb, b1B_sb):
        def _bct(t, dst=dst):
            t.wait_ge(sGrows, G_ROWS)
            t.wait_ge(sDv, em.n(sDv))
            t.matmul(psF[:, 0:512], onesb_sb[0:1, :], dst[0:1, 0:512],
                     start=True, stop=True)
            em.mark(t.matmul(psF[:, 512:1024], onesb_sb[0:1, :], dst[0:1, 512:1024],
                             start=True, stop=True), sPe)
        block.tensor(_bct)

        def _bcv(v, dst=dst):
            v.wait_ge(sPe, em.n(sPe))
            em.mark(v.tensor_copy(dst[:], psF[:]), sDv)
        block.vector(_bcv)

    def _bca(t):
        t.wait_ge(sGrows, G_ROWS)
        t.wait_ge(sDv, em.n(sDv))
        em.mark(t.matmul(psF[:, 0:128], onesb_sb[0:1, :], attrB_sb[0:1, 0:128],
                         start=True, stop=True), sPe)
    block.tensor(_bca)

    def _bcav(v):
        v.wait_ge(sPe, em.n(sPe))
        em.mark(v.tensor_copy(attrB_sb[:], psF[:, 0:128]), sDv)
    block.vector(_bcav)

    ST_P0 = em.n(sSt)
    DV_P0 = em.n(sDv)
    S0.close()

    SA = Scope(nc)
    wrT_sb = SA.T("wrT_sb", [128, 8 * D], BF16)
    fr_sb = SA.T("fr_sb", [128, TILES * D], BF16)
    srT_sb = SA.T("srT_sb", [HA, TOK], F32)
    mloc_sb = SA.T("mloc_sb", [HA, L], F32)
    negm_sb = SA.T("negm_sb", [HA, L], F32)
    zloc_sb = SA.T("zloc_sb", [HA, L], F32)
    Sst_sb = SA.T("Sst_sb", [1, D], F32)
    xnF_sb = [SA.T(f"xnF{i}", [128, D], BF16) for i in range(2)]
    xbuf = [SA.T(f"xbuf{i}", [128, D], F32) for i in range(2)]
    sr_sb = SA.T("sr_sb", [128, HA], F32)
    e4_bf = SA.T("e4_bf", [128, C4 * HA], BF16)

    barrier(block, em, [(sSt, ST_P0), (sDv, DV_P0)])

    def _gwr(g):
        for k in range(8):
            em.mark(g.dma_start(out=wrT_sb[:, k * D:(k + 1) * D],
                                in_=wrT[k * 128:(k + 1) * 128, :]), sGwr, 16)
    block.gpsimd(_gwr)
    G_WR = em.n(sGwr)

    def _x0(sync):
        em.mark(sync.dma_start(out=xbuf[0][:], in_=x_in[0:128, :]), sLdX[0], 16)
    block.sync(_x0)

    se_xn = []
    x_ld = [em.n(sLdX[0])]
    for t_i in range(TILES):
        buf = t_i % 2

        def _pf(sync, t_i=t_i, buf=buf):
            if t_i + 1 < TILES:
                if t_i >= 1:
                    sync.wait_ge(sSe, se_xn[t_i - 1])
                em.mark(sync.dma_start(out=xbuf[1 - buf][:],
                                       in_=x_in[(t_i + 1) * 128:(t_i + 2) * 128, :]),
                        sLdX[1 - buf], 16)
                x_ld.append(em.n(sLdX[1 - buf]))
        block.sync(_pf)

        def _w(v, t_i=t_i, buf=buf):
            v.wait_ge(sLdX[buf], x_ld[t_i])
        block.vector(_w)

        _ln_chain(block, em, sDv, sSe, xbuf[buf][:], xn_sb[:, t_i * D:(t_i + 1) * D], ln)
        se_xn.append(em.n(sSe))

        def _xnst(sync, t_i=t_i):
            sync.wait_ge(sSe, se_xn[t_i])
            em.mark(sync.dma_start(out=o_xn[t_i * 128:(t_i + 1) * 128, :],
                                   in_=xn_sb[:, t_i * D:(t_i + 1) * D]), sSt, 16)
        block.sync(_xnst)

        def _tr(t, t_i=t_i):
            t.wait_ge(sSe, se_xn[t_i])
            t.wait_ge(sDv, em.n(sDv))
            for k in range(8):
                i = t.transpose(psT[:, k * 128:(k + 1) * 128],
                                xn_sb[:, t_i * D + k * 128:t_i * D + (k + 1) * 128],
                                idb_sb[:])
            em.mark(i, sPe)
        block.tensor(_tr)

        def _trv(v, buf=buf):
            v.wait_ge(sPe, em.n(sPe))
            em.mark(v.tensor_copy(xnF_sb[buf][:], psT[:]), sDv)
        block.vector(_trv)

        def _fr(t, buf=buf):
            t.wait_ge(sGwr, G_WR)
            t.wait_ge(sDv, em.n(sDv))
            for k in range(8):
                t.matmul(psA[:, :], xnF_sb[buf][:, k * 128:(k + 1) * 128],
                         wrT_sb[:, k * D:k * D + 512], start=(k == 0), stop=(k == 7))
            for k in range(8):
                i = t.matmul(psB[:, :], xnF_sb[buf][:, k * 128:(k + 1) * 128],
                             wrT_sb[:, k * D + 512:(k + 1) * D],
                             start=(k == 0), stop=(k == 7))
            em.mark(i, sPe)
        block.tensor(_fr)

        def _frv(v, t_i=t_i):
            v.wait_ge(sPe, em.n(sPe))
            em.mark(v.tensor_copy(fr_sb[:, t_i * D:t_i * D + 512], psA[:]), sDv)
            nn = em.mark(v.tensor_copy(fr_sb[:, t_i * D + 512:(t_i + 1) * D], psB[:]), sDv)
            v.wait_ge(sDv, nn)
            nn = em.mark(v.scalar_tensor_tensor(
                bf2k[:], fr_sb[:, t_i * D:(t_i + 1) * D], 0.01,
                fr_sb[:, t_i * D:(t_i + 1) * D], op0=ALU.mult, op1=ALU.max), sDv)
            v.wait_ge(sDv, nn)
            for h in range(HA):
                em.mark(v.scalar_tensor_tensor(
                    dummyDV[:, 0:128], bf2k[:, h * 128:(h + 1) * 128], 1.0,
                    attrB_sb[:, 0:128], op0=ALU.mult, op1=ALU.mult,
                    accum_out=sr_sb[:, h:h + 1]), sDv)
        block.vector(_frv)

        def _srT(t, t_i=t_i):
            t.wait_ge(sDv, em.n(sDv))
            em.mark(t.transpose(psF[0:HA, 0:128], sr_sb[:], idf_sb[:]), sPe)
        block.tensor(_srT)

        def _srTv(v, t_i=t_i):
            v.wait_ge(sPe, em.n(sPe))
            em.mark(v.tensor_copy(srT_sb[:, t_i * 128:(t_i + 1) * 128],
                                  psF[0:HA, 0:128]), sDv)
        block.vector(_srTv)

    def _p2a(v):
        v.wait_ge(sDv, em.n(sDv))
        n0 = em.mark(v.reduce_max(
            mloc_sb[:], srT_sb[:].rearrange("h (l n) -> h l n", l=L), axis=AX.X), sDv)
        v.wait_ge(sDv, n0)
        em.mark(v.tensor_scalar(negm_sb[:], mloc_sb[:], -1.0, None, op0=ALU.mult), sDv)
    block.vector(_p2a)

    def _p2m(sync):
        sync.wait_ge(sDv, em.n(sDv))
        em.mark(sync.dma_start(out=o_m, in_=mloc_sb[:]), sSt, 16)
    block.sync(_p2m)

    def _p2e(s):
        s.wait_ge(sDv, em.n(sDv))
        for l in range(L):
            em.mark(s.activation(srT_sb[:, l * NLOC:(l + 1) * NLOC],
                                 srT_sb[:, l * NLOC:(l + 1) * NLOC], AF.Exp,
                                 bias=negm_sb[:, l:l + 1], scale=1.0), sSe)
    block.scalar(_p2e)
    SE_ET = em.n(sSe)

    def _p2z(v):
        v.wait_ge(sSe, SE_ET)
        em.mark(v.reduce_sum(
            zloc_sb[:], srT_sb[:].rearrange("h (l n) -> h l n", l=L), axis=AX.X), sDv)
    block.vector(_p2z)

    def _p2zs(sync):
        sync.wait_ge(sDv, em.n(sDv))
        em.mark(sync.dma_start(out=o_z, in_=zloc_sb[:]), sSt, 16)
    block.sync(_p2zs)

    for l in range(L):
        for c4 in range(C4):
            t_i = l * C4 + c4

            def _etr(t, t_i=t_i):
                t.wait_ge(sSe, SE_ET)
                t.wait_ge(sDv, em.n(sDv))
                em.mark(t.transpose(psF[:, 0:HA], srT_sb[0:HA, t_i * 128:(t_i + 1) * 128],
                                    idf_sb[0:HA, 0:HA]), sPe)
            block.tensor(_etr)

            def _etv(v, c4=c4):
                v.wait_ge(sPe, em.n(sPe))
                em.mark(v.tensor_copy(e4_bf[:, c4 * HA:(c4 + 1) * HA], psF[:, 0:HA]), sDv)
            block.vector(_etv)

        def _smm(t, l=l):
            t.wait_ge(sDv, em.n(sDv))
            for h in range(HA):
                for c4 in range(C4):
                    t_i = l * C4 + c4
                    i = t.matmul(psS[0:1, h * 128:(h + 1) * 128],
                                 e4_bf[:, c4 * HA + h:c4 * HA + h + 1],
                                 fr_sb[:, t_i * D + h * 128:t_i * D + (h + 1) * 128],
                                 start=(c4 == 0), stop=(c4 == C4 - 1),
                                 skip_group_check=True)
            em.mark(i, sPe)
        block.tensor(_smm)

        def _sev(v, l=l):
            v.wait_ge(sPe, em.n(sPe))
            if l >= 1:
                v.wait_ge(sSt, em.n(sSt))
            em.mark(v.tensor_copy(Sst_sb[0:1, :], psS[0:1, :]), sDv)
        block.vector(_sev)

        def _sst(sync, l=l):
            sync.wait_ge(sDv, em.n(sDv))
            em.mark(sync.dma_start(out=o_s[0:1, l * D:(l + 1) * D],
                                   in_=Sst_sb[:]), sSt, 16)
        block.sync(_sst)

    ST_ATTN = em.n(sSt)
    DV_ATTN = em.n(sDv)
    PE_ATTN = em.n(sPe)
    SA.close()

    SP = Scope(nc)
    ipT_sb = SP.T("ipT_sb", [128, 8 * 3 * 512], BF16)
    ipb_sb = SP.T("ipb_sb", [1, 3 * 512], BF16)
    opT_sb = SP.T("opT_sb", [128, 8 * D], BF16)
    opb_sb = SP.T("opb_sb", [1, D], BF16)
    qkv_sb = SP.T("qkv_sb", [128, L * 3 * D], BF16)
    xnFall = SP.T("xnFall", [128, L * D], BF16)
    oacc_sb = SP.T("oacc_sb", [128, D], F32)
    oF_sb = SP.T("oF_sb", [128, D], BF16)
    scr_sc = SP.T("scr_sc", [128, HM, 8], F32)
    scre_sc = SP.T("scre_sc", [128, HM, 8], F32)
    mxm_sc = SP.T("mxm_sc", [128, HM], F32)
    negmx_sc = SP.T("negmx_sc", [128, HM], F32)
    zm_sc = SP.T("zm_sc", [128, HM], F32)
    zinv_sc = SP.T("zinv_sc", [128, HM], F32)
    zsum_sb = SP.T("zsum_sb", [128, D], F32)
    xhat_sb = SP.T("xhat_sb", [128, D], F32)
    y1bf_sb = SP.T("y1bf_sb", [128, D], BF16)
    logits_sb = SP.T("logits_sb", [128, E], F32)
    mx8_sb = SP.T("mx8_sb", [128, E], F32)
    idxu_sb = SP.T("idxu_sb", [128, E], U32)
    e8_sb = SP.T("e8_sb", [128, E], F32)
    z8_sb = SP.T("z8_sb", [128, 1], F32)
    oh_sb = SP.T("oh_sb", [128, E], F32)

    barrier(block, em, [(sSt, ST_ATTN), (sDv, DV_ATTN), (sPe, PE_ATTN)])

    def _gop(g):
        for k in range(8):
            em.mark(g.dma_start(out=opT_sb[:, k * D:(k + 1) * D],
                                in_=opT[k * 128:(k + 1) * 128, :]), sGop, 16)
        em.mark(g.dma_start(out=opb_sb[:], in_=opb), sGop, 16)
    block.gpsimd(_gop)
    G_OP = em.n(sGop)

    for c4 in range(C4):
        for l in range(L):
            t_i = l * C4 + c4

            def _qtr(t, t_i=t_i, l=l):
                t.wait_ge(sDv, em.n(sDv))
                for k in range(8):
                    i = t.transpose(psT[:, k * 128:(k + 1) * 128],
                                    xn_sb[:, t_i * D + k * 128:t_i * D + (k + 1) * 128],
                                    idb_sb[:])
                em.mark(i, sPe)
            block.tensor(_qtr)

            def _qtrv(v, l=l):
                v.wait_ge(sPe, em.n(sPe))
                em.mark(v.tensor_copy(xnFall[:, l * D:(l + 1) * D], psT[:]), sDv)
            block.vector(_qtrv)

        for half in range(2):
            def _ghalf(g, half=half, c4=c4):
                if not (c4 == 0 and half == 0):
                    g.wait_ge(sPe, em.n(sPe))
                for k in range(8):
                    em.mark(g.dma_start(
                        out=ipT_sb[:, k * 1536:(k + 1) * 1536],
                        in_=ipT[k * 128:(k + 1) * 128,
                                half * 1536:(half + 1) * 1536]), sGip, 16)
                em.mark(g.dma_start(out=ipb_sb[:],
                                    in_=ipb[0:1, half * 1536:(half + 1) * 1536]), sGip, 16)
            block.gpsimd(_ghalf)
            g_half = em.n(sGip)

            for l in range(L):
                for ns in range(3):
                    nseg = half * 3 + ns
                    tgt = psA if ns % 2 == 0 else psB

                    def _qmm(t, l=l, ns=ns, nseg=nseg, tgt=tgt, g_half=g_half):
                        t.wait_ge(sGip, g_half)
                        t.wait_ge(sDv, em.n(sDv))
                        for k in range(8):
                            t.matmul(tgt[:, :], xnFall[:, l * D + k * 128:
                                                       l * D + (k + 1) * 128],
                                     ipT_sb[:, k * 1536 + ns * 512:
                                            k * 1536 + (ns + 1) * 512],
                                     start=(k == 0), stop=False)
                        em.mark(t.matmul(tgt[:, :], onesb_sb[0:1, 0:128],
                                         ipb_sb[0:1, ns * 512:(ns + 1) * 512],
                                         start=False, stop=True), sPe)
                    block.tensor(_qmm)

                    def _qev(v, l=l, nseg=nseg, tgt=tgt):
                        v.wait_ge(sPe, em.n(sPe))
                        em.mark(v.tensor_copy(
                            qkv_sb[:, l * 3 * D + nseg * 512:
                                   l * 3 * D + (nseg + 1) * 512], tgt[:, :]), sDv)
                    block.vector(_qev)

        for l in range(L):
            t_i = l * C4 + c4

            def _sc(v, l=l):
                v.wait_ge(sDv, em.n(sDv))
                for m in range(L):
                    for h in range(HM):
                        em.mark(v.scalar_tensor_tensor(
                            dummyDV[:, 0:256],
                            qkv_sb[:, l * 3 * D + h * 256:l * 3 * D + (h + 1) * 256],
                            0.0625,
                            qkv_sb[:, m * 3 * D + D + h * 256:
                                   m * 3 * D + D + (h + 1) * 256],
                            op0=ALU.mult, op1=ALU.mult,
                            accum_out=scr_sc[:, h, m:m + 1]), sDv)
                nn = em.n(sDv)
                v.wait_ge(sDv, nn)
                n0 = em.mark(v.reduce_max(mxm_sc[:], scr_sc[:], axis=AX.X), sDv)
                v.wait_ge(sDv, n0)
                em.mark(v.tensor_scalar(negmx_sc[:], mxm_sc[:], -1.0, None,
                                        op0=ALU.mult), sDv)
            block.vector(_sc)

            def _sce(s):
                s.wait_ge(sDv, em.n(sDv))
                for h in range(HM):
                    em.mark(s.activation(scre_sc[:, h, :], scr_sc[:, h, :], AF.Exp,
                                         bias=negmx_sc[:, h:h + 1], scale=1.0), sSe)
            block.scalar(_sce)

            def _so(v, l=l):
                v.wait_ge(sSe, em.n(sSe))
                n0 = em.mark(v.reduce_sum(zm_sc[:], scre_sc[:], axis=AX.X), sDv)
                v.wait_ge(sDv, n0)
                n1 = em.mark(v.reciprocal(zinv_sc[:], zm_sc[:]), sDv)
                v.wait_ge(sDv, n1)
                for h in range(HM):
                    em.mark(v.tensor_scalar(scre_sc[:, h, :], scre_sc[:, h, :],
                                            zinv_sc[:, h:h + 1], None, op0=ALU.mult), sDv)
                nn = em.n(sDv)
                v.wait_ge(sDv, nn)
                marks = {}
                for m in range(L):
                    for h in range(HM):
                        oseg = oacc_sb[:, h * 256:(h + 1) * 256]
                        vseg = qkv_sb[:, m * 3 * D + 2 * D + h * 256:
                                      m * 3 * D + 2 * D + (h + 1) * 256]
                        if m > 0:
                            v.wait_ge(sDv, marks[(m - 1, h)])
                        if m == 0:
                            marks[(m, h)] = em.mark(v.tensor_scalar(
                                oseg, vseg, scre_sc[:, h, m:m + 1], None,
                                op0=ALU.mult), sDv)
                        else:
                            marks[(m, h)] = em.mark(v.scalar_tensor_tensor(
                                oseg, vseg, scre_sc[:, h, m:m + 1], oseg,
                                op0=ALU.mult, op1=ALU.add), sDv)
                nn = em.n(sDv)
                v.wait_ge(sDv, nn)
                em.mark(v.tensor_copy(bf2k[:], oacc_sb[:]), sDv)
            block.vector(_so)

            def _otr(t):
                t.wait_ge(sDv, em.n(sDv))
                for k in range(8):
                    i = t.transpose(psT[:, k * 128:(k + 1) * 128],
                                    bf2k[:, k * 128:(k + 1) * 128], idb_sb[:])
                em.mark(i, sPe)
            block.tensor(_otr)

            def _otrv(v):
                v.wait_ge(sPe, em.n(sPe))
                em.mark(v.tensor_copy(oF_sb[:], psT[:]), sDv)
            block.vector(_otrv)

            def _omm(t):
                t.wait_ge(sGop, G_OP)
                t.wait_ge(sDv, em.n(sDv))
                for k in range(8):
                    t.matmul(psA[:, :], oF_sb[:, k * 128:(k + 1) * 128],
                             opT_sb[:, k * D:k * D + 512], start=(k == 0), stop=False)
                t.matmul(psA[:, :], onesb_sb[0:1, 0:128], opb_sb[0:1, 0:512],
                         start=False, stop=True)
                for k in range(8):
                    t.matmul(psB[:, :], oF_sb[:, k * 128:(k + 1) * 128],
                             opT_sb[:, k * D + 512:(k + 1) * D],
                             start=(k == 0), stop=False)
                em.mark(t.matmul(psB[:, :], onesb_sb[0:1, 0:128], opb_sb[0:1, 512:1024],
                                 start=False, stop=True), sPe)
            block.tensor(_omm)

            def _zs(v, t_i=t_i):
                v.wait_ge(sPe, em.n(sPe))
                em.mark(v.tensor_tensor(zsum_sb[:, 0:512],
                                        xn_sb[:, t_i * D:t_i * D + 512], psA[:, :],
                                        op=ALU.add), sDv)
                em.mark(v.tensor_tensor(zsum_sb[:, 512:1024],
                                        xn_sb[:, t_i * D + 512:(t_i + 1) * D], psB[:, :],
                                        op=ALU.add), sDv)
            block.vector(_zs)

            _ln_chain(block, em, sDv, sSe, zsum_sb[:], xhat_sb[:], ln)

            def _y1(v, t_i=t_i):
                v.wait_ge(sSe, em.n(sSe))
                v.wait_ge(sSt, em.n(sSt))
                n0 = em.mark(v.tensor_tensor(y1bf_sb[:], xhat_sb[:], g1B_sb[:],
                                             op=ALU.mult), sDv)
                v.wait_ge(sDv, n0)
                em.mark(v.tensor_tensor(y1bf_sb[:], y1bf_sb[:], b1B_sb[:],
                                        op=ALU.add), sDv)
            block.vector(_y1)

            def _y1st(sync, t_i=t_i):
                sync.wait_ge(sDv, em.n(sDv))
                em.mark(sync.dma_start(out=o_y1[t_i * 128:(t_i + 1) * 128, :],
                                       in_=y1bf_sb[:]), sSt, 16)
            block.sync(_y1st)

            def _xtr(t):
                t.wait_ge(sSe, em.n(sSe))
                t.wait_ge(sDv, em.n(sDv))
                for k in range(8):
                    i = t.transpose(psF[:, k * 128:(k + 1) * 128],
                                    xhat_sb[:, k * 128:(k + 1) * 128], idf_sb[:])
                em.mark(i, sPe)
            block.tensor(_xtr)

            def _xtrv(v):
                v.wait_ge(sPe, em.n(sPe))
                em.mark(v.tensor_copy(ln["dummy"][:], psF[:]), sDv)
            block.vector(_xtrv)

            def _lg(t):
                t.wait_ge(sDv, em.n(sDv))
                for k in range(8):
                    t.matmul(psF[0:128, 0:E], ln["dummy"][:, k * 128:(k + 1) * 128],
                             wgp_sb[:, k * E:(k + 1) * E],
                             start=(k == 0), stop=False, skip_group_check=True)
                em.mark(t.matmul(psF[0:128, 0:E], onesf_sb[0:1, 0:128], cvec_sb[0:1, :],
                                 start=False, stop=True, skip_group_check=True), sPe)
            block.tensor(_lg)

            def _am(v):
                v.wait_ge(sPe, em.n(sPe))
                n0 = em.mark(v.tensor_copy(logits_sb[:], psF[0:128, 0:E]), sDv)
                v.wait_ge(sDv, n0)
                n1 = em.mark(v.max(mx8_sb[:], logits_sb[:]), sDv)
                v.wait_ge(sDv, n1)
                n2 = em.mark(v.max_index(idxu_sb[:], mx8_sb[:], logits_sb[:]), sDv)
                v.wait_ge(sDv, n2)
                em.mark(v.tensor_scalar(ln["negmu"][:], mx8_sb[:, 0:1], -1.0, None,
                                        op0=ALU.mult), sDv)
            block.vector(_am)

            def _cv(s, t_i=t_i):
                s.wait_ge(sDv, em.n(sDv))
                em.mark(s.activation(idx_sb[:, t_i:t_i + 1], idxu_sb[:, 0:1],
                                     AF.Identity, bias=0.0, scale=1.0), sSe)
                em.mark(s.activation(e8_sb[:], logits_sb[:], AF.Exp,
                                     bias=ln["negmu"][:, 0:1], scale=1.0), sSe)
            block.scalar(_cv)

            def _gt(v, t_i=t_i):
                v.wait_ge(sSe, em.n(sSe))
                n0 = em.mark(v.reduce_sum(z8_sb[:], e8_sb[:], axis=AX.X), sDv)
                v.wait_ge(sDv, n0)
                n1 = em.mark(v.reciprocal(gate_sb[:, t_i:t_i + 1], z8_sb[:]), sDv)
                v.wait_ge(sDv, n1)
                em.mark(v.tensor_scalar(oh_sb[:], iotaE_sb[:], idx_sb[:, t_i:t_i + 1],
                                        0.0, op0=ALU.subtract, op1=ALU.is_equal), sDv)
            block.vector(_gt)

            def _cnt(t, t_i=t_i):
                t.wait_ge(sDv, em.n(sDv))
                em.mark(t.matmul(psC[0:1, t_i * E:(t_i + 1) * E], onescol_sb[:],
                                 oh_sb[:], start=True, stop=True,
                                 skip_group_check=True), sPe)
            block.tensor(_cnt)

    def _cfin(v):
        v.wait_ge(sPe, em.n(sPe))
        for l in range(L):
            v.wait_ge(sDv, em.n(sDv))
            em.mark(v.tensor_copy(cnt_sb[0:1, l * E:(l + 1) * E],
                                  psC[0:1, (l * C4) * E:(l * C4) * E + E]), sDv)
            for c4 in range(1, C4):
                v.wait_ge(sDv, em.n(sDv))
                em.mark(v.tensor_tensor(
                    cnt_sb[0:1, l * E:(l + 1) * E],
                    cnt_sb[0:1, l * E:(l + 1) * E],
                    psC[0:1, (l * C4 + c4) * E:(l * C4 + c4) * E + E],
                    op=ALU.add), sDv)
    block.vector(_cfin)

    def _fin(sync):
        sync.wait_ge(sDv, em.n(sDv))
        em.mark(sync.dma_start(out=o_cnt, in_=cnt_sb[:]), sSt, 16)
        em.mark(sync.dma_start(out=o_idx, in_=idx_sb[:]), sSt, 16)
        em.mark(sync.dma_start(out=o_gate, in_=gate_sb[:]), sSt, 16)
        sync.wait_ge(sSt, em.n(sSt))
    block.sync(_fin)

    SP.close()
    G.close()
    es.close()
    return nc



def build_launch2():
    nc = bass.Bass("TRN2", target_bir_lowering=False)

    x_in = nc.dram_tensor("x", [TOK, D], F32, kind="ExternalInput").ap()
    xn_in = nc.dram_tensor("xn", [TOK, D], BF16, kind="ExternalInput").ap()
    y1_in = nc.dram_tensor("y1", [TOK, D], BF16, kind="ExternalInput").ap()
    idx_in = nc.dram_tensor("idxv", [128, TILES], F32, kind="ExternalInput").ap()
    gate_in = nc.dram_tensor("gatev", [128, TILES], F32, kind="ExternalInput").ap()
    mz_in = nc.dram_tensor("mzall", [CORES, 2 * HA * L], F32, kind="ExternalInput").ap()
    s_in = nc.dram_tensor("sall", [CORES, L * D], F32, kind="ExternalInput").ap()
    cnt_in = nc.dram_tensor("cntall", [CORES * L, E], F32, kind="ExternalInput").ap()
    wcm_in = nc.dram_tensor("wcm", [CORES * L, L], F32, kind="ExternalInput").ap()
    w12_in = nc.dram_tensor("w12all", [E * D, D], BF16, kind="ExternalInput").ap()
    b12_in = nc.dram_tensor("b12all", [E, D], F32, kind="ExternalInput").ap()
    wfT_in = nc.dram_tensor("wfT", [D, D], F32, kind="ExternalInput").ap()
    lng_in = nc.dram_tensor("lng", [1, D], F32, kind="ExternalInput").ap()
    lnb_in = nc.dram_tensor("lnb", [1, D], F32, kind="ExternalInput").ap()
    g2_in = nc.dram_tensor("g2r", [1, D], F32, kind="ExternalInput").ap()
    b2r_in = nc.dram_tensor("b2r", [1, D], F32, kind="ExternalInput").ap()
    idf = nc.dram_tensor("idf", [128, 128], F32, kind="ExternalInput").ap()
    idb = nc.dram_tensor("idb", [128, 128], BF16, kind="ExternalInput").ap()
    onesrow_b = nc.dram_tensor("onesrow_b", [1, 128], BF16, kind="ExternalInput").ap()
    onesrow_f = nc.dram_tensor("onesrow_f", [1, 128], F32, kind="ExternalInput").ap()
    onescol_f = nc.dram_tensor("onescol_f", [128, 1], F32, kind="ExternalInput").ap()
    ltri_in = nc.dram_tensor("ltri", [128, 128], F32, kind="ExternalInput").ap()
    iotaE_in = nc.dram_tensor("iotaE", [128, E], F32, kind="ExternalInput").ap()

    o_out = nc.dram_tensor("o_out", [TOK, D], F32, kind="ExternalOutput").ap()

    staging = nc.dram_tensor("staging", [E * CPC, D], BF16)
    outstage = nc.dram_tensor("outstage", [E * CPC, D], BF16)

    from contextlib import ExitStack
    es = ExitStack()
    P = lambda name, shape, dt: es.enter_context(nc.psum_tensor(name, shape, dt))

    G = Scope(nc)
    xn_sb = G.T("xn_sb", [128, TILES * D], BF16)
    y1_sb = G.T("y1_sb", [128, TILES * D], BF16)
    fhB_sb = G.T("fhB_sb", [128, L * D], BF16)
    lngB = G.T("lngB", [128, D], BF16)
    lnbB = G.T("lnbB", [128, D], BF16)
    g2B = G.T("g2B", [128, D], BF16)
    b2B = G.T("b2B", [128, D], BF16)
    idf_sb = G.T("idf_sb", [128, 128], F32)
    idb_sb = G.T("idb_sb", [128, 128], BF16)
    onesb_sb = G.T("onesb_sb", [1, 128], BF16)
    onesf_sb = G.T("onesf_sb", [1, 128], F32)
    onescol_sb = G.T("onescol_sb", [128, 1], F32)
    ltri_sb = G.T("ltri_sb", [128, 128], F32)
    iotaE_sb = G.T("iotaE_sb", [128, E], F32)
    idx_sb = G.T("idx_sb", [128, TILES], F32)
    gate_sb = G.T("gate_sb", [128, TILES], F32)
    slotu_sb = G.T("slotu_sb", [128, TILES], U32)
    cs_sb = G.T("cs_sb", [1, TILES * E], F32)
    runl_sb = G.T("runl_sb", [1, TILES * E], F32)
    runa_sb = G.T("runa_sb", [1, TILES * E], F32)
    zero_sb = G.T("zero_sb", [128, D], BF16)
    ln = {k: G.T("ln_" + k, [128, 1], F32) for k in
          ("sx", "negmu", "svar", "vep", "rstd", "nmr")}
    ln["dummy"] = G.T("ln_dummy", [128, D], F32)
    smalls = {k: G.T("sm_" + k, [128, 1], F32) for k in
              ("loc", "lpos", "keep", "gv", "slot", "ovf")}
    oh_sb = G.T("oh_sb", [128, E], F32)
    ohrun_sb = G.T("ohrun_sb", [128, E], F32)

    psA = P("psA", [128, 512], F32)
    psB = P("psB", [128, 512], F32)
    psT = P("psT", [128, D], BF16)
    psF = P("psF", [128, D], F32)
    psS = P("psS", [1, D], F32)
    psC = P("psC", [L, 512], F32)

    sLd = es.enter_context(nc.semaphore("sLd"))
    sG = es.enter_context(nc.semaphore("sG"))
    sGz = es.enter_context(nc.semaphore("sGz"))
    sGsc = es.enter_context(nc.semaphore("sGsc"))
    sGb = es.enter_context(nc.semaphore("sGb"))
    sGg = es.enter_context(nc.semaphore("sGg"))
    sLdsl = es.enter_context(nc.semaphore("sLdsl"))
    sLdfh = es.enter_context(nc.semaphore("sLdfh"))
    sLdb = es.enter_context(nc.semaphore("sLdb"))
    sLdw = es.enter_context(nc.semaphore("sLdw"))
    sLdst = es.enter_context(nc.semaphore("sLdst"))
    sLdX = [es.enter_context(nc.semaphore(f"sLdX{i}")) for i in range(2)]
    sPe = es.enter_context(nc.semaphore("sPe"))
    sDv = es.enter_context(nc.semaphore("sDv"))
    sSe = es.enter_context(nc.semaphore("sSe"))
    sSt = es.enter_context(nc.semaphore("sSt"))
    block = es.enter_context(nc.Block())
    em = Emit()

    SA = Scope(nc)
    wbig_sb = SA.T("wbig_sb", [128, 8 * D], BF16)
    mz_sb = SA.T("mz_sb", [CORES, 2 * HA * L], F32)
    mzT_sb = SA.T("mzT_sb", [HA * L, CORES], F32)
    zT0_sb = SA.T("zT0_sb", [HA * L, CORES], F32)
    mg_sb = SA.T("mg_sb", [HA * L, 1], F32)
    negmg_sb = SA.T("negmg_sb", [HA * L, 1], F32)
    wT_sb = SA.T("wT_sb", [HA * L, CORES], F32)
    zt_sb = SA.T("zt_sb", [HA * L, CORES], F32)
    zg_sb = SA.T("zg_sb", [HA * L, 1], F32)
    zgi_sb = SA.T("zgi_sb", [HA * L, 1], F32)
    w8_sb = SA.T("w8_sb", [CORES, HA * L], F32)
    s_l = SA.T("s_l", [CORES, D], F32)
    ctx_l = SA.T("ctx_l", [1, D], F32)
    ctxF_sb = SA.T("ctxF_sb", [128, 8 * L], BF16)
    fh_sb = SA.T("fh_sb", [L, D], F32)
    fhrow_sb = SA.T("fhrow_sb", [1, D], F32)
    cnt2_sb = SA.T("cnt2_sb", [CORES * L, E], F32)
    wcm_sb = SA.T("wcm_sb", [CORES * L, L], F32)
    base_sb = SA.T("base_sb", [L, E], F32)
    baser_sb = SA.T("baser_sb", [1, L * E], F32)
    stage_sb = SA.T("stage_sb", [128, D], BF16)
    stF_sb = SA.T("stF_sb", [128, D], BF16)
    eo_sb = SA.T("eo_sb", [128, D], BF16)
    b12e_sb = SA.T("b12e_sb", [1, D], BF16)

    def _ld(sync):
        for t_i in range(TILES):
            em.mark(sync.dma_start(out=xn_sb[:, t_i * D:(t_i + 1) * D],
                                   in_=xn_in[t_i * 128:(t_i + 1) * 128, :]), sLd, 16)
            em.mark(sync.dma_start(out=y1_sb[:, t_i * D:(t_i + 1) * D],
                                   in_=y1_in[t_i * 128:(t_i + 1) * 128, :]), sLd, 16)
        for dst, src in ((idf_sb[:], idf), (idb_sb[:], idb), (onesb_sb[:], onesrow_b),
                         (onesf_sb[:], onesrow_f), (onescol_sb[:], onescol_f),
                         (ltri_sb[:], ltri_in), (iotaE_sb[:], iotaE_in),
                         (idx_sb[:], idx_in), (gate_sb[:], gate_in),
                         (mz_sb[:], mz_in), (wcm_sb[:], wcm_in), (cnt2_sb[:], cnt_in)):
            em.mark(sync.dma_start(out=dst, in_=src), sLd, 16)
    block.sync(_ld)
    LD0 = em.n(sLd)

    def _gld(g):
        for k in range(8):
            em.mark(g.dma_start(out=wbig_sb[:, k * D:(k + 1) * D],
                                in_=wfT_in[k * 128:(k + 1) * 128, :]), sG, 16)
        for dst, src in ((lngB, lng_in), (lnbB, lnb_in), (g2B, g2_in), (b2B, b2r_in)):
            em.mark(g.dma_start(out=dst[0:1, :], in_=src), sG, 16)
    block.gpsimd(_gld)
    G0 = em.n(sG)

    def _z(v):
        em.mark(v.memset(zero_sb[:], 0.0), sDv)
    block.vector(_z)

    bc_cell = []

    def _zs(g):
        bc_cell.append(g.to_reg(E * CPC - 1))
        g.wait_ge(sDv, em.n(sDv))
        for r in range(E * CPC // 128):
            em.mark(g.dma_start(out=staging.ap()[r * 128:(r + 1) * 128, :],
                                in_=zero_sb[:]), sGz, 16)
    block.gpsimd(_zs)
    G_ZERO = em.n(sGz)

    for dst in (lngB, lnbB, g2B, b2B):
        def _bct(t, dst=dst):
            t.wait_ge(sG, G0)
            t.wait_ge(sLd, LD0)
            t.wait_ge(sDv, em.n(sDv))
            t.matmul(psF[:, 0:512], onesb_sb[0:1, :], dst[0:1, 0:512],
                     start=True, stop=True)
            em.mark(t.matmul(psF[:, 512:1024], onesb_sb[0:1, :], dst[0:1, 512:1024],
                             start=True, stop=True), sPe)
        block.tensor(_bct)

        def _bcv(v, dst=dst):
            v.wait_ge(sPe, em.n(sPe))
            em.mark(v.tensor_copy(dst[:], psF[:]), sDv)
        block.vector(_bcv)

    def _t1(t):
        t.wait_ge(sDv, em.n(sDv))
        em.mark(t.transpose(psF[0:128, 0:CORES], mz_sb[:], idf_sb[0:CORES, 0:CORES]), sPe)
    block.tensor(_t1)

    def _v1(v):
        v.wait_ge(sPe, em.n(sPe))
        em.mark(v.tensor_copy(mzT_sb[:], psF[0:HA * L, 0:CORES]), sDv)
        n0 = em.mark(v.tensor_copy(zT0_sb[:], psF[HA * L:2 * HA * L, 0:CORES]), sDv)
        v.wait_ge(sDv, n0)
        n1 = em.mark(v.reduce_max(mg_sb[:], mzT_sb[0:HA * L, :], axis=AX.X), sDv)
        v.wait_ge(sDv, n1)
        em.mark(v.tensor_scalar(negmg_sb[:], mg_sb[:], -1.0, None, op0=ALU.mult), sDv)
    block.vector(_v1)

    def _s1(s):
        s.wait_ge(sDv, em.n(sDv))
        em.mark(s.activation(wT_sb[:], mzT_sb[0:HA * L, :], AF.Exp,
                             bias=negmg_sb[:, 0:1], scale=1.0), sSe)
    block.scalar(_s1)

    def _v2(v):
        v.wait_ge(sSe, em.n(sSe))
        n0 = em.mark(v.tensor_tensor(zt_sb[:], wT_sb[:], zT0_sb[:], op=ALU.mult), sDv)
        v.wait_ge(sDv, n0)
        n1 = em.mark(v.reduce_sum(zg_sb[:], zt_sb[:], axis=AX.X), sDv)
        v.wait_ge(sDv, n1)
        n2 = em.mark(v.reciprocal(zgi_sb[:], zg_sb[:]), sDv)
        v.wait_ge(sDv, n2)
        em.mark(v.tensor_scalar(wT_sb[:], wT_sb[:], zgi_sb[:, 0:1], None,
                                op0=ALU.mult), sDv)
    block.vector(_v2)

    def _t2(t):
        t.wait_ge(sDv, em.n(sDv))
        em.mark(t.transpose(psF[0:CORES, 0:HA * L], wT_sb[:],
                            idf_sb[0:HA * L, 0:HA * L]), sPe)
    block.tensor(_t2)

    def _v3(v):
        v.wait_ge(sPe, em.n(sPe))
        em.mark(v.tensor_copy(w8_sb[:], psF[0:CORES, 0:HA * L]), sDv)
    block.vector(_v3)

    for l in range(L):
        def _sll(sync, l=l):
            if l >= 1:
                sync.wait_ge(sPe, em.n(sPe))
            em.mark(sync.dma_start(out=s_l[:], in_=s_in[:, l * D:(l + 1) * D]), sLdsl, 16)
        block.sync(_sll)
        ld_sl = em.n(sLdsl)

        def _ctx(t, l=l, ld_sl=ld_sl):
            t.wait_ge(sLdsl, ld_sl)
            t.wait_ge(sDv, em.n(sDv))
            for h in range(HA):
                i = t.matmul(psS[0:1, h * 128:(h + 1) * 128],
                             w8_sb[:, l * HA + h:l * HA + h + 1],
                             s_l[:, h * 128:(h + 1) * 128],
                             start=True, stop=True, skip_group_check=True)
            em.mark(i, sPe)
        block.tensor(_ctx)

        def _ctxv(v, l=l):
            v.wait_ge(sPe, em.n(sPe))
            em.mark(v.tensor_copy(ctx_l[0:1, :], psS[0:1, :]), sDv)
        block.vector(_ctxv)

        for k in range(8):
            def _cf(t, l=l, k=k):
                t.wait_ge(sDv, em.n(sDv))
                em.mark(t.transpose(psF[:, 0:1],
                                    ctx_l[0:1, k * 128:(k + 1) * 128],
                                    idf_sb[0:1, 0:1]), sPe)
            block.tensor(_cf)

            def _cfv(v, l=l, k=k):
                v.wait_ge(sPe, em.n(sPe))
                em.mark(v.tensor_copy(ctxF_sb[:, k * L + l:k * L + l + 1],
                                      psF[:, 0:1]), sDv)
            block.vector(_cfv)

    def _fh(t):
        t.wait_ge(sG, G0)
        t.wait_ge(sDv, em.n(sDv))
        for k in range(8):
            t.matmul(psA[0:L, :], ctxF_sb[:, k * L:(k + 1) * L],
                     wbig_sb[:, k * D:k * D + 512], start=(k == 0), stop=(k == 7))
        for k in range(8):
            i = t.matmul(psB[0:L, :], ctxF_sb[:, k * L:(k + 1) * L],
                         wbig_sb[:, k * D + 512:(k + 1) * D],
                         start=(k == 0), stop=(k == 7))
        em.mark(i, sPe)
    block.tensor(_fh)
    PE_FH = em.n(sPe)

    def _fhv(v):
        v.wait_ge(sPe, PE_FH)
        em.mark(v.tensor_copy(fh_sb[:, 0:512], psA[0:L, :]), sDv)
        em.mark(v.tensor_copy(fh_sb[:, 512:1024], psB[0:L, :]), sDv)
    block.vector(_fhv)
    DV_FH = em.n(sDv)

    for l in range(L):
        def _fhr(sync, l=l):
            sync.wait_ge(sDv, DV_FH)
            if l >= 1:
                sync.wait_ge(sPe, em.n(sPe))
            em.mark(sync.dma_start(out=fhrow_sb[0:1, :], in_=fh_sb[l:l + 1, :]), sLdfh, 16)
        block.sync(_fhr)
        ld_f = em.n(sLdfh)

        def _fb(t, l=l, ld_f=ld_f):
            t.wait_ge(sLdfh, ld_f)
            t.wait_ge(sDv, em.n(sDv))
            t.matmul(psA[:, :], onesf_sb[0:1, :], fhrow_sb[0:1, 0:512],
                     start=True, stop=True)
            em.mark(t.matmul(psB[:, :], onesf_sb[0:1, :], fhrow_sb[0:1, 512:1024],
                             start=True, stop=True), sPe)
        block.tensor(_fb)

        def _fbv(v, l=l):
            v.wait_ge(sPe, em.n(sPe))
            em.mark(v.tensor_copy(fhB_sb[:, l * D:l * D + 512], psA[:]), sDv)
            em.mark(v.tensor_copy(fhB_sb[:, l * D + 512:(l + 1) * D], psB[:]), sDv)
        block.vector(_fbv)

    def _base(t):
        t.wait_ge(sDv, em.n(sDv))
        em.mark(t.matmul(psC[0:L, 0:E], wcm_sb[:], cnt2_sb[:],
                         start=True, stop=True), sPe)
    block.tensor(_base)

    def _basev(v):
        v.wait_ge(sPe, em.n(sPe))
        em.mark(v.tensor_copy(base_sb[:], psC[0:L, 0:E]), sDv)
    block.vector(_basev)

    def _baser(sync):
        sync.wait_ge(sDv, em.n(sDv))
        em.mark(sync.dma_start(out=baser_sb[:], in_=base_sb[:]), sLdb, 16)
    block.sync(_baser)
    LD_BASE = em.n(sLdb)

    for t_i in range(TILES):
        def _oh(v, t_i=t_i):
            v.wait_ge(sPe, em.n(sPe))
            v.wait_ge(sDv, em.n(sDv))
            em.mark(v.tensor_scalar(oh_sb[:], iotaE_sb[:], idx_sb[:, t_i:t_i + 1],
                                    0.0, op0=ALU.subtract, op1=ALU.is_equal), sDv)
        block.vector(_oh)

        def _cs(t, t_i=t_i):
            t.wait_ge(sDv, em.n(sDv))
            em.mark(t.matmul(psC[0:1, t_i * E:(t_i + 1) * E], onescol_sb[:], oh_sb[:],
                             start=True, stop=True, skip_group_check=True), sPe)
        block.tensor(_cs)

    def _csv(v):
        v.wait_ge(sPe, em.n(sPe))
        em.mark(v.tensor_copy(cs_sb[0:1, :], psC[0:1, 0:TILES * E]), sDv)
    block.vector(_csv)

    def _pre(v):
        v.wait_ge(sDv, em.n(sDv))
        v.wait_ge(sLdb, LD_BASE)
        for t_i in range(TILES):
            l, c4 = divmod(t_i, C4)
            if c4 == 0:
                em.mark(v.tensor_copy(runl_sb[0:1, t_i * E:(t_i + 1) * E],
                                      baser_sb[0:1, l * E:(l + 1) * E]), sDv)
            else:
                v.wait_ge(sDv, em.n(sDv))
                em.mark(v.tensor_tensor(runl_sb[0:1, t_i * E:(t_i + 1) * E],
                                        runl_sb[0:1, (t_i - 1) * E:t_i * E],
                                        cs_sb[0:1, (t_i - 1) * E:t_i * E],
                                        op=ALU.add), sDv)
            if t_i == 0:
                em.mark(v.memset(runa_sb[0:1, 0:E], 0.0), sDv)
            else:
                v.wait_ge(sDv, em.n(sDv))
                em.mark(v.tensor_tensor(runa_sb[0:1, t_i * E:(t_i + 1) * E],
                                        runa_sb[0:1, (t_i - 1) * E:t_i * E],
                                        cs_sb[0:1, (t_i - 1) * E:t_i * E],
                                        op=ALU.add), sDv)
    block.vector(_pre)

    for t_i in range(TILES):
        def _oh2(v, t_i=t_i):
            v.wait_ge(sDv, em.n(sDv))
            v.wait_ge(sPe, em.n(sPe))
            em.mark(v.tensor_scalar(oh_sb[:], iotaE_sb[:], idx_sb[:, t_i:t_i + 1],
                                    0.0, op0=ALU.subtract, op1=ALU.is_equal), sDv)
        block.vector(_oh2)

        def _lt(t, t_i=t_i):
            t.wait_ge(sDv, em.n(sDv))
            t.matmul(psF[0:128, 0:E], ltri_sb[:], oh_sb[:],
                     start=True, stop=False, skip_group_check=True)
            t.matmul(psF[0:128, 0:E], onesf_sb[0:1, 0:128],
                     runl_sb[0:1, t_i * E:(t_i + 1) * E],
                     start=False, stop=True, skip_group_check=True)
            t.matmul(psF[0:128, E:2 * E], ltri_sb[:], oh_sb[:],
                     start=True, stop=False, skip_group_check=True)
            em.mark(t.matmul(psF[0:128, E:2 * E], onesf_sb[0:1, 0:128],
                             runa_sb[0:1, t_i * E:(t_i + 1) * E],
                             start=False, stop=True, skip_group_check=True), sPe)
        block.tensor(_lt)

        def _sel(v, t_i=t_i):
            v.wait_ge(sPe, em.n(sPe))
            n0 = em.mark(v.tensor_tensor(ohrun_sb[:], psF[0:128, 0:E], oh_sb[:],
                                         op=ALU.mult), sDv)
            v.wait_ge(sDv, n0)
            n1 = em.mark(v.reduce_sum(smalls["loc"][:], ohrun_sb[:], axis=AX.X), sDv)
            v.wait_ge(sDv, n1)
            n2 = em.mark(v.tensor_tensor(ohrun_sb[:], psF[0:128, E:2 * E], oh_sb[:],
                                         op=ALU.mult), sDv)
            v.wait_ge(sDv, n2)
            n3 = em.mark(v.reduce_sum(smalls["lpos"][:], ohrun_sb[:], axis=AX.X), sDv)
            v.wait_ge(sDv, n3)
            n4 = em.mark(v.tensor_scalar(smalls["keep"][:], smalls["loc"][:], float(CAP),
                                         None, op0=ALU.is_lt), sDv)
            v.wait_ge(sDv, n4)
            em.mark(v.tensor_tensor(smalls["gv"][:], smalls["keep"][:],
                                    gate_sb[:, t_i:t_i + 1], op=ALU.mult), sDv)
            n6 = em.mark(v.tensor_scalar(smalls["slot"][:], idx_sb[:, t_i:t_i + 1],
                                         float(CPC), smalls["lpos"][:, 0:1],
                                         op0=ALU.mult, op1=ALU.add), sDv)
            n7 = em.mark(v.tensor_scalar(smalls["ovf"][:], smalls["lpos"][:], float(CPC),
                                         None, op0=ALU.is_ge), sDv)
            v.wait_ge(sDv, n7)
            n8 = em.mark(v.scalar_tensor_tensor(smalls["slot"][:], smalls["ovf"][:], OOB,
                                                smalls["slot"][:], op0=ALU.mult,
                                                op1=ALU.add), sDv)
            v.wait_ge(sDv, n8)
            em.mark(v.tensor_copy(gate_sb[:, t_i:t_i + 1], smalls["gv"][:]), sDv)
        block.vector(_sel)

        def _cvt(s, t_i=t_i):
            s.wait_ge(sDv, em.n(sDv))
            em.mark(s.activation(slotu_sb[:, t_i:t_i + 1], smalls["slot"][:], AF.Identity,
                                 bias=0.0, scale=1.0), sSe)
        block.scalar(_cvt)

        def _scat(g, t_i=t_i):
            g.wait_ge(sSe, em.n(sSe))
            g.wait_ge(sGz, G_ZERO)
            em.mark(g.indirect_dma_start(
                out=staging.ap()[0:128, :],
                out_offset=IndirectOffsetOnAxis(slotu_sb[:, t_i:t_i + 1], axis=0),
                in_=y1_sb[:, t_i * D:(t_i + 1) * D],
                in_offset=None,
                bounds_check=bc_cell[0],
                oob_is_err=False), sGsc, 16)
        block.gpsimd(_scat)

    G_SCAT = em.n(sGsc)

    for e in range(E):
        def _wld(sync, e=e):
            if e == 0:
                sync.wait_ge(sGsc, G_SCAT)
                sync.wait_ge(sPe, PE_FH)
            else:
                sync.wait_ge(sPe, em.n(sPe))
            for k in range(8):
                em.mark(sync.dma_start(
                    out=wbig_sb[:, k * D:(k + 1) * D],
                    in_=w12_in[e * D + k * 128:e * D + (k + 1) * 128, :]), sLdw, 16)
        block.sync(_wld)
        ld_w = em.n(sLdw)

        def _bld(g, e=e):
            if e == 0:
                g.wait_ge(sGsc, G_SCAT)
            else:
                g.wait_ge(sPe, em.n(sPe))
            em.mark(g.dma_start(out=b12e_sb[0:1, :], in_=b12_in[e:e + 1, :]), sGb, 16)
        block.gpsimd(_bld)
        g_b = em.n(sGb)

        for r in range(CPC // 128):
            def _sld(sync, e=e, r=r):
                sync.wait_ge(sPe, em.n(sPe))
                em.mark(sync.dma_start(
                    out=stage_sb[:],
                    in_=staging.ap()[e * CPC + r * 128:e * CPC + (r + 1) * 128, :]),
                    sLdst, 16)
            block.sync(_sld)
            ld_s = em.n(sLdst)

            def _str(t, ld_s=ld_s):
                t.wait_ge(sLdst, ld_s)
                t.wait_ge(sDv, em.n(sDv))
                for k in range(8):
                    i = t.transpose(psT[:, k * 128:(k + 1) * 128],
                                    stage_sb[:, k * 128:(k + 1) * 128], idb_sb[:])
                em.mark(i, sPe)
            block.tensor(_str)

            def _stv(v):
                v.wait_ge(sPe, em.n(sPe))
                em.mark(v.tensor_copy(stF_sb[:], psT[:]), sDv)
            block.vector(_stv)

            def _emm(t, e=e, ld_w=ld_w, g_b=g_b):
                t.wait_ge(sLdw, ld_w)
                t.wait_ge(sGb, g_b)
                t.wait_ge(sDv, em.n(sDv))
                for k in range(8):
                    t.matmul(psA[:, :], stF_sb[:, k * 128:(k + 1) * 128],
                             wbig_sb[:, k * D:k * D + 512], start=(k == 0), stop=False)
                t.matmul(psA[:, :], onesb_sb[0:1, 0:128], b12e_sb[0:1, 0:512],
                         start=False, stop=True)
                for k in range(8):
                    t.matmul(psB[:, :], stF_sb[:, k * 128:(k + 1) * 128],
                             wbig_sb[:, k * D + 512:(k + 1) * D],
                             start=(k == 0), stop=False)
                em.mark(t.matmul(psB[:, :], onesb_sb[0:1, 0:128], b12e_sb[0:1, 512:1024],
                                 start=False, stop=True), sPe)
            block.tensor(_emm)

            def _eev(v):
                v.wait_ge(sPe, em.n(sPe))
                v.wait_ge(sSt, em.n(sSt))
                em.mark(v.tensor_copy(eo_sb[:, 0:512], psA[:]), sDv)
                em.mark(v.tensor_copy(eo_sb[:, 512:1024], psB[:]), sDv)
            block.vector(_eev)

            def _est(sync, e=e, r=r):
                sync.wait_ge(sDv, em.n(sDv))
                em.mark(sync.dma_start(
                    out=outstage.ap()[e * CPC + r * 128:e * CPC + (r + 1) * 128, :],
                    in_=eo_sb[:]), sSt, 16)
            block.sync(_est)

    ST_EXP = em.n(sSt)
    DV_EXP = em.n(sDv)
    SA.close()

    SC = Scope(nc)
    xbuf = [SC.T(f"xbuf{i}", [128, D], F32) for i in range(2)]
    gz_sb = SC.T("gz_sb", [128, D], BF16)
    moe_sb = SC.T("moe_sb", [128, D], F32)
    z2_sb = SC.T("z2_sb", [128, D], F32)
    xh2_sb = SC.T("xh2_sb", [128, D], F32)
    y2_sb = SC.T("y2_sb", [128, D], F32)
    aout_sb = SC.T("aout_sb", [128, D], F32)
    out_sb = SC.T("out_sb", [128, D], F32)

    barrier(block, em, [(sSt, ST_EXP), (sDv, DV_EXP)])

    for t_i in range(TILES):
        l = t_i // C4
        buf = t_i % 2

        def _xl(sync, t_i=t_i, buf=buf):
            if t_i >= 2:
                sync.wait_ge(sDv, em.n(sDv))
            em.mark(sync.dma_start(out=xbuf[buf][:],
                                   in_=x_in[t_i * 128:(t_i + 1) * 128, :]), sLdX[buf], 16)
        block.sync(_xl)
        ldx = em.n(sLdX[buf])

        def _gzz(v, t_i=t_i):
            v.wait_ge(sGg, em.n(sGg))
            v.wait_ge(sDv, em.n(sDv))
            em.mark(v.memset(gz_sb[:], 0.0), sDv)
        block.vector(_gzz)

        def _gth(g, t_i=t_i):
            g.wait_ge(sSt, ST_EXP)
            g.wait_ge(sDv, em.n(sDv))
            em.mark(g.indirect_dma_start(
                out=gz_sb[:],
                out_offset=None,
                in_=outstage.ap()[0:128, :],
                in_offset=IndirectOffsetOnAxis(slotu_sb[:, t_i:t_i + 1], axis=0),
                bounds_check=bc_cell[0],
                oob_is_err=False), sGg, 16)
        block.gpsimd(_gth)
        gn = em.n(sGg)

        def _moe(v, t_i=t_i, gn=gn):
            v.wait_ge(sGg, gn)
            n0 = em.mark(v.tensor_scalar(moe_sb[:], gz_sb[:], gate_sb[:, t_i:t_i + 1],
                                         None, op0=ALU.mult), sDv)
            v.wait_ge(sDv, n0)
            em.mark(v.tensor_tensor(z2_sb[:], y1_sb[:, t_i * D:(t_i + 1) * D],
                                    moe_sb[:], op=ALU.add), sDv)
        block.vector(_moe)

        _ln_chain(block, em, sDv, sSe, z2_sb[:], xh2_sb[:], ln)

        def _y2(v):
            v.wait_ge(sSe, em.n(sSe))
            n0 = em.mark(v.tensor_tensor(y2_sb[:], xh2_sb[:], g2B[:], op=ALU.mult), sDv)
            v.wait_ge(sDv, n0)
            em.mark(v.tensor_tensor(y2_sb[:], y2_sb[:], b2B[:], op=ALU.add), sDv)
        block.vector(_y2)

        def _ya(v, t_i=t_i, l=l):
            v.wait_ge(sDv, em.n(sDv))
            em.mark(v.tensor_tensor(z2_sb[:], xn_sb[:, t_i * D:(t_i + 1) * D],
                                    fhB_sb[:, l * D:(l + 1) * D], op=ALU.add), sDv)
        block.vector(_ya)

        _ln_chain(block, em, sDv, sSe, z2_sb[:], xh2_sb[:], ln)

        def _ao(v):
            v.wait_ge(sSe, em.n(sSe))
            n0 = em.mark(v.tensor_tensor(aout_sb[:], xh2_sb[:], lngB[:], op=ALU.mult), sDv)
            v.wait_ge(sDv, n0)
            em.mark(v.tensor_tensor(aout_sb[:], aout_sb[:], lnbB[:], op=ALU.add), sDv)
        block.vector(_ao)

        def _sum(v, buf=buf, ldx=ldx):
            v.wait_ge(sLdX[buf], ldx)
            v.wait_ge(sDv, em.n(sDv))
            v.wait_ge(sSt, em.n(sSt))
            n0 = em.mark(v.tensor_tensor(out_sb[:], xbuf[buf][:], aout_sb[:],
                                         op=ALU.add), sDv)
            v.wait_ge(sDv, n0)
            em.mark(v.tensor_tensor(out_sb[:], out_sb[:], y2_sb[:], op=ALU.add), sDv)
        block.vector(_sum)

        def _ost(sync, t_i=t_i):
            sync.wait_ge(sDv, em.n(sDv))
            em.mark(sync.dma_start(out=o_out[t_i * 128:(t_i + 1) * 128, :],
                                   in_=out_sb[:]), sSt, 16)
        block.sync(_ost)

    def _end(sync):
        sync.wait_ge(sSt, em.n(sSt))
    block.sync(_end)

    SC.close()
    G.close()
    es.close()
    return nc



def _consts():
    idf = np.eye(128, dtype=np.float32)
    idb = np.eye(128, dtype=BF16NP)
    onesrow_b = np.ones((1, 128), dtype=BF16NP)
    onesrow_f = np.ones((1, 128), dtype=np.float32)
    onescol_f = np.ones((128, 1), dtype=np.float32)
    iotaE = np.tile(np.arange(E, dtype=np.float32), (128, 1))
    ltri = np.triu(np.ones((128, 128), dtype=np.float32), 1)
    return idf, idb, onesrow_b, onesrow_f, onescol_f, iotaE, ltri


def kernel(L0_attn_h, L0_attn_Wl, L0_attn_Wr, L0_attn_att_l, L0_attn_att_r,
           L0_attn_Wf, L0_attn_ln_gamma, L0_attn_ln_beta, L0_ffn_in_proj_w,
           L0_ffn_in_proj_b, L0_ffn_out_proj_w, L0_ffn_out_proj_b, L0_ffn_ln1_g,
           L0_ffn_ln1_b, L0_ffn_ln2_g, L0_ffn_ln2_b, L0_ffn_wg, L0_ffn_w1,
           L0_ffn_b1, L0_ffn_w2, L0_ffn_b2):
    f32 = np.float32
    x = np.asarray(L0_attn_h, f32)
    idf, idb, onesrow_b, onesrow_f, onescol_f, iotaE, ltri = _consts()

    wrT = np.ascontiguousarray(np.asarray(L0_attn_Wr, f32).T)
    attr = np.ascontiguousarray(np.asarray(L0_attn_att_r, f32).reshape(1, 128))
    ipT = np.ascontiguousarray(np.asarray(L0_ffn_in_proj_w, f32).T)
    ipb = np.asarray(L0_ffn_in_proj_b, f32).reshape(1, 3 * D)
    opT = np.ascontiguousarray(np.asarray(L0_ffn_out_proj_w, f32).T)
    opb = np.asarray(L0_ffn_out_proj_b, f32).reshape(1, D)
    g1 = np.asarray(L0_ffn_ln1_g, f32).reshape(1, D)
    b1r = np.asarray(L0_ffn_ln1_b, f32).reshape(1, D)
    wgA = np.ascontiguousarray(np.asarray(L0_ffn_wg, f32))
    w1 = np.asarray(L0_ffn_w1, f32)
    b1 = np.asarray(L0_ffn_b1, f32)
    w2 = np.asarray(L0_ffn_w2, f32)
    b2 = np.asarray(L0_ffn_b2, f32)
    wfT = np.ascontiguousarray(np.asarray(L0_attn_Wf, f32).T)
    lng = np.asarray(L0_attn_ln_gamma, f32).reshape(1, D)
    lnb = np.asarray(L0_attn_ln_beta, f32).reshape(1, D)
    g2r = np.asarray(L0_ffn_ln2_g, f32).reshape(1, D)
    b2r = np.asarray(L0_ffn_ln2_b, f32).reshape(1, D)

    xs = [np.ascontiguousarray(x[:, c * NLOC:(c + 1) * NLOC, :].reshape(TOK, D))
          for c in range(CORES)]

    nc1 = build_launch1()
    in_maps1 = []
    for c in range(CORES):
        in_maps1.append({
            "x": xs[c], "wrT": wrT, "attr": attr, "ipT": ipT, "ipb": ipb,
            "opT": opT, "opb": opb, "g1": g1, "b1r": b1r, "wg": wgA,
            "w1T": np.ascontiguousarray(w1[c].T),
            "w2e": np.ascontiguousarray(w2[c]),
            "b1e": np.ascontiguousarray(b1[c].reshape(FF // 128, 128).T),
            "b2e": b2[c].reshape(1, D),
            "idf": idf, "idb": idb, "onesrow_b": onesrow_b,
            "onesrow_f": onesrow_f, "onescol_f": onescol_f, "iotaE": iotaE,
        })
    res1 = run_bass_kernel_spmd(nc1, in_maps1, core_ids=list(range(CORES)))
    r1 = res1.results

    mz_all = np.stack([
        np.concatenate([np.asarray(r1[c]["o_m"], f32).T.reshape(-1),
                        np.asarray(r1[c]["o_z"], f32).T.reshape(-1)])
        for c in range(CORES)], axis=0).astype(f32)
    s_all = np.ascontiguousarray(
        np.stack([np.asarray(r1[c]["o_s"], f32).reshape(-1) for c in range(CORES)]))
    cnt_all = np.ascontiguousarray(
        np.stack([np.asarray(r1[c]["o_cnt"], f32).reshape(L, E)
                  for c in range(CORES)]).reshape(CORES * L, E))
    w12_all = np.ascontiguousarray(
        np.concatenate([np.asarray(r1[c]["o_w12"]) for c in range(CORES)], axis=0))
    b12_all = np.ascontiguousarray(
        np.stack([np.asarray(r1[c]["o_b12"], f32).reshape(-1) for c in range(CORES)]))

    nc2 = build_launch2()
    in_maps2 = []
    for c in range(CORES):
        wcm = np.zeros((CORES * L, L), dtype=f32)
        for cp in range(CORES):
            for lp in range(L):
                for l in range(L):
                    if lp < l or (lp == l and cp < c):
                        wcm[cp * L + lp, l] = 1.0
        in_maps2.append({
            "x": xs[c],
            "xn": np.asarray(r1[c]["o_xn"]),
            "y1": np.asarray(r1[c]["o_y1"]),
            "idxv": np.asarray(r1[c]["o_idx"], f32),
            "gatev": np.asarray(r1[c]["o_gate"], f32),
            "mzall": mz_all, "sall": s_all, "cntall": cnt_all, "wcm": wcm,
            "w12all": w12_all, "b12all": b12_all,
            "wfT": wfT, "lng": lng, "lnb": lnb, "g2r": g2r, "b2r": b2r,
            "idf": idf, "idb": idb, "onesrow_b": onesrow_b,
            "onesrow_f": onesrow_f, "onescol_f": onescol_f,
            "ltri": ltri, "iotaE": iotaE,
        })
    res2 = run_bass_kernel_spmd(nc2, in_maps2, core_ids=list(range(CORES)))

    out = np.empty((L, NSEQ, D), dtype=f32)
    for c in range(CORES):
        oc = np.asarray(res2.results[c]["o_out"], f32).reshape(L, NLOC, D)
        out[:, c * NLOC:(c + 1) * NLOC, :] = oc
    return out


# version 03
# speedup vs baseline: 1.0117x; 1.0117x over previous
import numpy as np
import ml_dtypes
import concourse.bass as bass
import concourse.mybir as mybir
from concourse.bass import IndirectOffsetOnAxis
from concourse.bass_utils import run_bass_kernel_spmd

F32 = mybir.dt.float32
BF16 = mybir.dt.bfloat16
U32 = mybir.dt.uint32
AF = mybir.ActivationFunctionType
ALU = mybir.AluOpType
AX = mybir.AxisListType
BF16NP = ml_dtypes.bfloat16

CORES = 8
D = 1024
L = 8
NSEQ = 4096
NLOC = NSEQ // CORES
TOK = L * NLOC
TILES = TOK // 128
C4 = 4
HA = 8
HM = 4
E = 8
FF = 4096
CAP = 4096
CPC = 1024
EPS = 1e-5
OOB = float(1 << 24)


class Emit:
    def __init__(self):
        self.cnt = {}

    def mark(self, instr, sem, k=1):
        instr.then_inc(sem, k)
        self.cnt[sem.name] = self.cnt.get(sem.name, 0) + k
        return self.cnt[sem.name]

    def n(self, sem):
        return self.cnt.get(sem.name, 0)


class Scope:

    def __init__(self, nc):
        self.nc = nc
        self.cms = []

    def T(self, name, shape, dt):
        cm = self.nc.sbuf_tensor(name, shape, dt)
        t = cm.__enter__()
        self.cms.append(cm)
        return t

    def close(self):
        for cm in reversed(self.cms):
            cm.__exit__(None, None, None)
        self.cms = []


def barrier(block, em, sems_waits):
    def _b(e):
        for sem, val in sems_waits:
            e.wait_ge(sem, val)
    block.vector(_b)
    block.scalar(_b)
    block.tensor(_b)
    block.sync(_b)
    block.gpsimd(_b)


def _ln_chain(block, em, sDv, sSe, src, xhat_out, ln):
    def _v1(v):
        v.wait_ge(sDv, em.n(sDv))
        n0 = em.mark(v.reduce_sum(ln["sx"][:], src, axis=AX.X), sDv)
        v.wait_ge(sDv, n0)
        em.mark(v.tensor_scalar(ln["negmu"][:], ln["sx"][:], -1.0 / D, None, op0=ALU.mult), sDv)
    block.vector(_v1)

    def _s1(s):
        s.wait_ge(sDv, em.n(sDv))
        em.mark(s.activation(ln["dummy"][:], src, AF.Square, bias=ln["negmu"][:, 0:1],
                             scale=1.0, accum_out=ln["svar"][:]), sSe)
    block.scalar(_s1)

    def _v2(v):
        v.wait_ge(sSe, em.n(sSe))
        em.mark(v.tensor_scalar(ln["vep"][:], ln["svar"][:], 1.0 / D, EPS,
                                op0=ALU.mult, op1=ALU.add), sDv)
    block.vector(_v2)

    def _s2a(s):
        s.wait_ge(sDv, em.n(sDv))
        em.mark(s.activation(ln["rstd"][:], ln["vep"][:], AF.Ln, bias=0.0, scale=1.0), sSe)
    block.scalar(_s2a)

    def _v2b(v):
        v.wait_ge(sSe, em.n(sSe))
        em.mark(v.tensor_scalar(ln["rstd"][:], ln["rstd"][:], -0.5, None,
                                op0=ALU.mult), sDv)
    block.vector(_v2b)

    def _s2b(s):
        s.wait_ge(sDv, em.n(sDv))
        em.mark(s.activation(ln["rstd"][:], ln["rstd"][:], AF.Exp, bias=0.0, scale=1.0), sSe)
    block.scalar(_s2b)

    def _v3(v):
        v.wait_ge(sSe, em.n(sSe))
        em.mark(v.tensor_scalar(ln["nmr"][:], ln["negmu"][:], ln["rstd"][:, 0:1], None,
                                op0=ALU.mult), sDv)
    block.vector(_v3)

    def _s3(s):
        s.wait_ge(sDv, em.n(sDv))
        em.mark(s.activation(xhat_out, src, AF.Identity, bias=ln["nmr"][:, 0:1],
                             scale=ln["rstd"][:, 0:1]), sSe)
    block.scalar(_s3)
    return em.n(sSe)



def build_launch1():
    nc = bass.Bass("TRN2", target_bir_lowering=False)

    x_in = nc.dram_tensor("x", [TOK, D], F32, kind="ExternalInput").ap()
    wrT = nc.dram_tensor("wrT", [D, D], F32, kind="ExternalInput").ap()
    attr = nc.dram_tensor("attr", [1, 128], F32, kind="ExternalInput").ap()
    ipT = nc.dram_tensor("ipT", [D, 3 * D], F32, kind="ExternalInput").ap()
    ipb = nc.dram_tensor("ipb", [1, 3 * D], F32, kind="ExternalInput").ap()
    opT = nc.dram_tensor("opT", [D, D], F32, kind="ExternalInput").ap()
    opb = nc.dram_tensor("opb", [1, D], F32, kind="ExternalInput").ap()
    g1 = nc.dram_tensor("g1", [1, D], F32, kind="ExternalInput").ap()
    b1r = nc.dram_tensor("b1r", [1, D], F32, kind="ExternalInput").ap()
    wg = nc.dram_tensor("wg", [D, E], F32, kind="ExternalInput").ap()
    w1T = nc.dram_tensor("w1T", [FF, D], F32, kind="ExternalInput").ap()
    w2e = nc.dram_tensor("w2e", [FF, D], F32, kind="ExternalInput").ap()
    b1e = nc.dram_tensor("b1e", [128, FF // 128], F32, kind="ExternalInput").ap()
    b2e = nc.dram_tensor("b2e", [1, D], F32, kind="ExternalInput").ap()
    idf = nc.dram_tensor("idf", [128, 128], F32, kind="ExternalInput").ap()
    idb = nc.dram_tensor("idb", [128, 128], BF16, kind="ExternalInput").ap()
    onesrow_b = nc.dram_tensor("onesrow_b", [1, 128], BF16, kind="ExternalInput").ap()
    onesrow_f = nc.dram_tensor("onesrow_f", [1, 128], F32, kind="ExternalInput").ap()
    onescol_f = nc.dram_tensor("onescol_f", [128, 1], F32, kind="ExternalInput").ap()
    iotaE_in = nc.dram_tensor("iotaE", [128, E], F32, kind="ExternalInput").ap()

    o_w12 = nc.dram_tensor("o_w12", [D, D], BF16, kind="ExternalOutput").ap()
    o_b12 = nc.dram_tensor("o_b12", [1, D], F32, kind="ExternalOutput").ap()
    o_m = nc.dram_tensor("o_m", [HA, L], F32, kind="ExternalOutput").ap()
    o_z = nc.dram_tensor("o_z", [HA, L], F32, kind="ExternalOutput").ap()
    o_s = nc.dram_tensor("o_s", [1, L * D], F32, kind="ExternalOutput").ap()
    o_cnt = nc.dram_tensor("o_cnt", [1, L * E], F32, kind="ExternalOutput").ap()
    o_y1 = nc.dram_tensor("o_y1", [TOK, D], BF16, kind="ExternalOutput").ap()
    o_xn = nc.dram_tensor("o_xn", [TOK, D], BF16, kind="ExternalOutput").ap()
    o_idx = nc.dram_tensor("o_idx", [128, TILES], F32, kind="ExternalOutput").ap()
    o_gate = nc.dram_tensor("o_gate", [128, TILES], F32, kind="ExternalOutput").ap()

    from contextlib import ExitStack
    es = ExitStack()
    P = lambda name, shape, dt: es.enter_context(nc.psum_tensor(name, shape, dt))

    G = Scope(nc)
    xn_sb = G.T("xn_sb", [128, TILES * D], BF16)
    idf_sb = G.T("idf_sb", [128, 128], F32)
    idb_sb = G.T("idb_sb", [128, 128], BF16)
    onesb_sb = G.T("onesb_sb", [1, 128], BF16)
    onesf_sb = G.T("onesf_sb", [1, 128], F32)
    onescol_sb = G.T("onescol_sb", [128, 1], F32)
    iotaE_sb = G.T("iotaE_sb", [128, E], F32)
    attrB_sb = G.T("attrB_sb", [128, 128], BF16)
    wgp_sb = G.T("wgp_sb", [128, 8 * E], F32)
    cvec_sb = G.T("cvec_sb", [1, E], F32)
    g1B_sb = G.T("g1B_sb", [128, D], BF16)
    b1B_sb = G.T("b1B_sb", [128, D], BF16)
    idx_sb = G.T("idx_sb", [128, TILES], F32)
    gate_sb = G.T("gate_sb", [128, TILES], F32)
    cnt_sb = G.T("cnt_sb", [1, L * E], F32)
    bf2k = G.T("bf2k", [128, D], BF16)
    ln = {k: G.T("ln_" + k, [128, 1], F32) for k in
          ("sx", "negmu", "svar", "vep", "rstd", "nmr")}
    ln["dummy"] = G.T("ln_dummy", [128, D], F32)

    psA = P("psA", [128, 512], F32)
    psB = P("psB", [128, 512], F32)
    psT = P("psT", [128, D], BF16)
    psF = P("psF", [128, D], F32)
    psS = P("psS", [1, D], F32)
    psC = P("psC", [1, 512], F32)

    sLd = es.enter_context(nc.semaphore("sLd"))
    sGw2 = es.enter_context(nc.semaphore("sGw2"))
    sGw1 = es.enter_context(nc.semaphore("sGw1"))
    sGrows = es.enter_context(nc.semaphore("sGrows"))
    sGwr = es.enter_context(nc.semaphore("sGwr"))
    sGip = es.enter_context(nc.semaphore("sGip"))
    sGop = es.enter_context(nc.semaphore("sGop"))
    sLdX = [es.enter_context(nc.semaphore(f"sLdX{i}")) for i in range(2)]
    sPe = es.enter_context(nc.semaphore("sPe"))
    sDv = es.enter_context(nc.semaphore("sDv"))
    sSe = es.enter_context(nc.semaphore("sSe"))
    sSt = es.enter_context(nc.semaphore("sSt"))
    block = es.enter_context(nc.Block())
    em = Emit()

    S0 = Scope(nc)
    w2_sb = S0.T("w2_sb", [128, 32 * D], BF16)
    w1m_sb = S0.T("w1m_sb", [128, FF], BF16)
    b1c_sb = S0.T("b1c_sb", [128, FF // 128], BF16)
    w12_sb = S0.T("w12_sb", [128, D], BF16)
    b12_sb = S0.T("b12_sb", [1, D], F32)
    b2_sb = S0.T("b2_sb", [1, D], F32)
    g1f_sb = S0.T("g1f_sb", [1, D], F32)
    b1f_sb = S0.T("b1f_sb", [1, D], F32)
    wg_sb = S0.T("wg_sb", [128, 8 * E], F32)
    col_sb = S0.T("col_sb", [128, 8], F32)

    def _ld(sync):
        for dst, src in ((idf_sb[:], idf), (idb_sb[:], idb), (onesb_sb[:], onesrow_b),
                         (onesf_sb[:], onesrow_f), (onescol_sb[:], onescol_f),
                         (iotaE_sb[:], iotaE_in), (b2_sb[:], b2e),
                         (g1f_sb[:], g1), (b1f_sb[:], b1r)):
            em.mark(sync.dma_start(out=dst, in_=src), sLd, 16)
        for k in range(8):
            em.mark(sync.dma_start(out=wg_sb[:, k * E:(k + 1) * E],
                                   in_=wg[k * 128:(k + 1) * 128, :]), sLd, 16)
    block.sync(_ld)
    LD_PRE = em.n(sLd)

    def _gw2(g):
        for k in range(32):
            em.mark(g.dma_start(out=w2_sb[:, k * D:(k + 1) * D],
                                in_=w2e[k * 128:(k + 1) * 128, :]), sGw2, 16)
        em.mark(g.dma_start(out=b1c_sb[:], in_=b1e), sGw2, 16)
    block.gpsimd(_gw2)
    G_W2 = em.n(sGw2)

    pe_after_M = []
    for M in range(8):
        def _gw1(g, M=M):
            if M >= 1:
                g.wait_ge(sPe, pe_after_M[M - 1])
            em.mark(g.dma_start(
                out=w1m_sb[:].rearrange("p (k i) -> p k i", k=32),
                in_=w1T[:, M * 128:(M + 1) * 128].rearrange("(k p) i -> p k i", p=128)),
                sGw1, 16)
        block.gpsimd(_gw1)
        g_w1m = em.n(sGw1)

        def _tmm(t, M=M, g_w1m=g_w1m):
            t.wait_ge(sGw2, G_W2)
            t.wait_ge(sGw1, g_w1m)
            if M >= 1:
                t.wait_ge(sDv, em.n(sDv))
            for K in range(32):
                t.matmul(psA[:, :], w1m_sb[:, K * 128:(K + 1) * 128],
                         w2_sb[:, K * D:K * D + 512], start=(K == 0), stop=(K == 31))
            for K in range(32):
                i = t.matmul(psB[:, :], w1m_sb[:, K * 128:(K + 1) * 128],
                             w2_sb[:, K * D + 512:(K + 1) * D],
                             start=(K == 0), stop=(K == 31))
                if K == 31 and M != 0:
                    em.mark(i, sPe)
            if M == 0:
                for K in range(32):
                    t.matmul(psS[0:1, 0:512], b1c_sb[:, K:K + 1],
                             w2_sb[:, K * D:K * D + 512],
                             start=(K == 0), stop=(K == 31), skip_group_check=True)
                for K in range(32):
                    i = t.matmul(psS[0:1, 512:1024], b1c_sb[:, K:K + 1],
                                 w2_sb[:, K * D + 512:(K + 1) * D],
                                 start=(K == 0), stop=(K == 31), skip_group_check=True)
                    if K == 31:
                        em.mark(i, sPe)
        block.tensor(_tmm)
        pe_after_M.append(em.n(sPe))

        def _vev(v, M=M):
            v.wait_ge(sPe, pe_after_M[M])
            if M >= 1:
                v.wait_ge(sSt, em.n(sSt))
            em.mark(v.tensor_copy(w12_sb[:, 0:512], psA[:, :]), sDv)
            em.mark(v.tensor_copy(w12_sb[:, 512:1024], psB[:, :]), sDv)
        block.vector(_vev)

        def _st(sync, M=M):
            sync.wait_ge(sDv, em.n(sDv))
            em.mark(sync.dma_start(out=o_w12[M * 128:(M + 1) * 128, :],
                                   in_=w12_sb[:]), sSt, 16)
        block.sync(_st)

    def _b12(v):
        v.wait_ge(sPe, pe_after_M[0])
        v.wait_ge(sLd, LD_PRE)
        em.mark(v.tensor_tensor(b12_sb[0:1, 0:512], psS[0:1, 0:512],
                                b2_sb[0:1, 0:512], op=ALU.add), sDv)
        em.mark(v.tensor_tensor(b12_sb[0:1, 512:1024], psS[0:1, 512:1024],
                                b2_sb[0:1, 512:1024], op=ALU.add), sDv)
    block.vector(_b12)

    def _b12st(sync):
        sync.wait_ge(sDv, em.n(sDv))
        em.mark(sync.dma_start(out=o_b12, in_=b12_sb[:]), sSt, 16)
    block.sync(_b12st)

    def _g1c(t):
        t.wait_ge(sLd, LD_PRE)
        t.wait_ge(sPe, pe_after_M[-1])
        for k in range(8):
            i = t.transpose(psF[:, k:k + 1], g1f_sb[0:1, k * 128:(k + 1) * 128],
                            idf_sb[0:1, 0:1])
        em.mark(i, sPe)
    block.tensor(_g1c)

    def _g1cv(v):
        v.wait_ge(sPe, em.n(sPe))
        n0 = em.mark(v.tensor_copy(col_sb[:], psF[:, 0:8]), sDv)
        v.wait_ge(sDv, n0)
        for k in range(8):
            em.mark(v.tensor_scalar(wgp_sb[:, k * E:(k + 1) * E],
                                    wg_sb[:, k * E:(k + 1) * E],
                                    col_sb[:, k:k + 1], None, op0=ALU.mult), sDv)
    block.vector(_g1cv)

    def _b1c2(t):
        t.wait_ge(sDv, em.n(sDv))
        for k in range(8):
            i = t.transpose(psF[:, 8 + k:9 + k], b1f_sb[0:1, k * 128:(k + 1) * 128],
                            idf_sb[0:1, 0:1])
        em.mark(i, sPe)
    block.tensor(_b1c2)

    def _b1cv(v):
        v.wait_ge(sPe, em.n(sPe))
        em.mark(v.tensor_copy(col_sb[:], psF[:, 8:16]), sDv)
    block.vector(_b1cv)

    def _cvec(t):
        t.wait_ge(sDv, em.n(sDv))
        for k in range(8):
            i = t.matmul(psC[0:1, 0:E], col_sb[:, k:k + 1], wg_sb[:, k * E:(k + 1) * E],
                         start=(k == 0), stop=(k == 7))
        em.mark(i, sPe)
    block.tensor(_cvec)

    def _cvecv(v):
        v.wait_ge(sPe, em.n(sPe))
        em.mark(v.tensor_copy(cvec_sb[:], psC[0:1, 0:E]), sDv)
    block.vector(_cvecv)

    def _gldg(g):
        em.mark(g.dma_start(out=g1B_sb[0:1, :], in_=g1), sGrows, 16)
        em.mark(g.dma_start(out=b1B_sb[0:1, :], in_=b1r), sGrows, 16)
        em.mark(g.dma_start(out=attrB_sb[0:1, :], in_=attr), sGrows, 16)
    block.gpsimd(_gldg)
    G_ROWS = em.n(sGrows)

    for dst in (g1B_sb, b1B_sb):
        def _bct(t, dst=dst):
            t.wait_ge(sGrows, G_ROWS)
            t.wait_ge(sDv, em.n(sDv))
            t.matmul(psF[:, 0:512], onesb_sb[0:1, :], dst[0:1, 0:512],
                     start=True, stop=True)
            em.mark(t.matmul(psF[:, 512:1024], onesb_sb[0:1, :], dst[0:1, 512:1024],
                             start=True, stop=True), sPe)
        block.tensor(_bct)

        def _bcv(v, dst=dst):
            v.wait_ge(sPe, em.n(sPe))
            em.mark(v.tensor_copy(dst[:], psF[:]), sDv)
        block.vector(_bcv)

    def _bca(t):
        t.wait_ge(sGrows, G_ROWS)
        t.wait_ge(sDv, em.n(sDv))
        em.mark(t.matmul(psF[:, 0:128], onesb_sb[0:1, :], attrB_sb[0:1, 0:128],
                         start=True, stop=True), sPe)
    block.tensor(_bca)

    def _bcav(v):
        v.wait_ge(sPe, em.n(sPe))
        em.mark(v.tensor_copy(attrB_sb[:], psF[:, 0:128]), sDv)
    block.vector(_bcav)

    ST_P0 = em.n(sSt)
    DV_P0 = em.n(sDv)
    S0.close()

    SA = Scope(nc)
    wrT_sb = SA.T("wrT_sb", [128, 8 * D], BF16)
    fr_sb = SA.T("fr_sb", [128, TILES * D], BF16)
    srT_sb = SA.T("srT_sb", [HA, TOK], F32)
    mloc_sb = SA.T("mloc_sb", [HA, L], F32)
    negm_sb = SA.T("negm_sb", [HA, L], F32)
    zloc_sb = SA.T("zloc_sb", [HA, L], F32)
    Sst_sb = SA.T("Sst_sb", [1, D], F32)
    xnF_sb = [SA.T(f"xnF{i}", [128, D], BF16) for i in range(2)]
    xbuf = [SA.T(f"xbuf{i}", [128, D], F32) for i in range(2)]
    sr_sb = SA.T("sr_sb", [128, HA], F32)
    e4_bf = SA.T("e4_bf", [128, C4 * HA], BF16)

    barrier(block, em, [(sSt, ST_P0), (sDv, DV_P0)])

    def _gwr(g):
        for k in range(8):
            em.mark(g.dma_start(out=wrT_sb[:, k * D:(k + 1) * D],
                                in_=wrT[k * 128:(k + 1) * 128, :]), sGwr, 16)
    block.gpsimd(_gwr)
    G_WR = em.n(sGwr)

    def _x0(sync):
        em.mark(sync.dma_start(out=xbuf[0][:], in_=x_in[0:128, :]), sLdX[0], 16)
    block.sync(_x0)

    se_xn = []
    x_ld = [em.n(sLdX[0])]
    for t_i in range(TILES):
        buf = t_i % 2

        def _pf(sync, t_i=t_i, buf=buf):
            if t_i + 1 < TILES:
                if t_i >= 1:
                    sync.wait_ge(sSe, se_xn[t_i - 1])
                em.mark(sync.dma_start(out=xbuf[1 - buf][:],
                                       in_=x_in[(t_i + 1) * 128:(t_i + 2) * 128, :]),
                        sLdX[1 - buf], 16)
                x_ld.append(em.n(sLdX[1 - buf]))
        block.sync(_pf)

        def _w(v, t_i=t_i, buf=buf):
            v.wait_ge(sLdX[buf], x_ld[t_i])
        block.vector(_w)

        _ln_chain(block, em, sDv, sSe, xbuf[buf][:], xn_sb[:, t_i * D:(t_i + 1) * D], ln)
        se_xn.append(em.n(sSe))

        def _xnst(sync, t_i=t_i):
            sync.wait_ge(sSe, se_xn[t_i])
            em.mark(sync.dma_start(out=o_xn[t_i * 128:(t_i + 1) * 128, :],
                                   in_=xn_sb[:, t_i * D:(t_i + 1) * D]), sSt, 16)
        block.sync(_xnst)

        def _tr(t, t_i=t_i):
            t.wait_ge(sSe, se_xn[t_i])
            t.wait_ge(sDv, em.n(sDv))
            for k in range(8):
                i = t.transpose(psT[:, k * 128:(k + 1) * 128],
                                xn_sb[:, t_i * D + k * 128:t_i * D + (k + 1) * 128],
                                idb_sb[:])
            em.mark(i, sPe)
        block.tensor(_tr)

        def _trv(v, buf=buf):
            v.wait_ge(sPe, em.n(sPe))
            em.mark(v.tensor_copy(xnF_sb[buf][:], psT[:]), sDv)
        block.vector(_trv)

        def _fr(t, buf=buf):
            t.wait_ge(sGwr, G_WR)
            t.wait_ge(sDv, em.n(sDv))
            for k in range(8):
                t.matmul(psA[:, :], xnF_sb[buf][:, k * 128:(k + 1) * 128],
                         wrT_sb[:, k * D:k * D + 512], start=(k == 0), stop=(k == 7))
            for k in range(8):
                i = t.matmul(psB[:, :], xnF_sb[buf][:, k * 128:(k + 1) * 128],
                             wrT_sb[:, k * D + 512:(k + 1) * D],
                             start=(k == 0), stop=(k == 7))
            em.mark(i, sPe)
        block.tensor(_fr)

        def _frv(v, t_i=t_i):
            v.wait_ge(sPe, em.n(sPe))
            em.mark(v.tensor_copy(fr_sb[:, t_i * D:t_i * D + 512], psA[:]), sDv)
            nn = em.mark(v.tensor_copy(fr_sb[:, t_i * D + 512:(t_i + 1) * D], psB[:]), sDv)
            v.wait_ge(sDv, nn)
            nn = em.mark(v.scalar_tensor_tensor(
                bf2k[:], fr_sb[:, t_i * D:(t_i + 1) * D], 0.01,
                fr_sb[:, t_i * D:(t_i + 1) * D], op0=ALU.mult, op1=ALU.max), sDv)
            v.wait_ge(sDv, nn)
            for h in range(HA):
                em.mark(v.scalar_tensor_tensor(
                    ln["dummy"][:, h * 128:(h + 1) * 128],
                    bf2k[:, h * 128:(h + 1) * 128], 1.0,
                    attrB_sb[:, 0:128], op0=ALU.mult, op1=ALU.mult,
                    accum_out=sr_sb[:, h:h + 1]), sDv)
        block.vector(_frv)

        def _srT(t, t_i=t_i):
            t.wait_ge(sDv, em.n(sDv))
            em.mark(t.transpose(psF[0:HA, 0:128], sr_sb[:], idf_sb[:]), sPe)
        block.tensor(_srT)

        def _srTv(v, t_i=t_i):
            v.wait_ge(sPe, em.n(sPe))
            em.mark(v.tensor_copy(srT_sb[:, t_i * 128:(t_i + 1) * 128],
                                  psF[0:HA, 0:128]), sDv)
        block.vector(_srTv)

    def _p2a(v):
        v.wait_ge(sDv, em.n(sDv))
        n0 = em.mark(v.reduce_max(
            mloc_sb[:], srT_sb[:].rearrange("h (l n) -> h l n", l=L), axis=AX.X), sDv)
        v.wait_ge(sDv, n0)
        em.mark(v.tensor_scalar(negm_sb[:], mloc_sb[:], -1.0, None, op0=ALU.mult), sDv)
    block.vector(_p2a)

    def _p2m(sync):
        sync.wait_ge(sDv, em.n(sDv))
        em.mark(sync.dma_start(out=o_m, in_=mloc_sb[:]), sSt, 16)
    block.sync(_p2m)

    def _p2e(s):
        s.wait_ge(sDv, em.n(sDv))
        for l in range(L):
            em.mark(s.activation(srT_sb[:, l * NLOC:(l + 1) * NLOC],
                                 srT_sb[:, l * NLOC:(l + 1) * NLOC], AF.Exp,
                                 bias=negm_sb[:, l:l + 1], scale=1.0), sSe)
    block.scalar(_p2e)
    SE_ET = em.n(sSe)

    def _p2z(v):
        v.wait_ge(sSe, SE_ET)
        em.mark(v.reduce_sum(
            zloc_sb[:], srT_sb[:].rearrange("h (l n) -> h l n", l=L), axis=AX.X), sDv)
    block.vector(_p2z)

    def _p2zs(sync):
        sync.wait_ge(sDv, em.n(sDv))
        em.mark(sync.dma_start(out=o_z, in_=zloc_sb[:]), sSt, 16)
    block.sync(_p2zs)

    for l in range(L):
        for c4 in range(C4):
            t_i = l * C4 + c4

            def _etr(t, t_i=t_i):
                t.wait_ge(sSe, SE_ET)
                t.wait_ge(sDv, em.n(sDv))
                em.mark(t.transpose(psF[:, 0:HA], srT_sb[0:HA, t_i * 128:(t_i + 1) * 128],
                                    idf_sb[0:HA, 0:HA]), sPe)
            block.tensor(_etr)

            def _etv(v, c4=c4):
                v.wait_ge(sPe, em.n(sPe))
                em.mark(v.tensor_copy(e4_bf[:, c4 * HA:(c4 + 1) * HA], psF[:, 0:HA]), sDv)
            block.vector(_etv)

        def _smm(t, l=l):
            t.wait_ge(sDv, em.n(sDv))
            for h in range(HA):
                for c4 in range(C4):
                    t_i = l * C4 + c4
                    i = t.matmul(psS[0:1, h * 128:(h + 1) * 128],
                                 e4_bf[:, c4 * HA + h:c4 * HA + h + 1],
                                 fr_sb[:, t_i * D + h * 128:t_i * D + (h + 1) * 128],
                                 start=(c4 == 0), stop=(c4 == C4 - 1),
                                 skip_group_check=True)
            em.mark(i, sPe)
        block.tensor(_smm)

        def _sev(v, l=l):
            v.wait_ge(sPe, em.n(sPe))
            if l >= 1:
                v.wait_ge(sSt, em.n(sSt))
            em.mark(v.tensor_copy(Sst_sb[0:1, :], psS[0:1, :]), sDv)
        block.vector(_sev)

        def _sst(sync, l=l):
            sync.wait_ge(sDv, em.n(sDv))
            em.mark(sync.dma_start(out=o_s[0:1, l * D:(l + 1) * D],
                                   in_=Sst_sb[:]), sSt, 16)
        block.sync(_sst)

    ST_ATTN = em.n(sSt)
    DV_ATTN = em.n(sDv)
    PE_ATTN = em.n(sPe)
    SA.close()

    SP = Scope(nc)
    ipT_sb = SP.T("ipT_sb", [128, 8 * 3 * 512], BF16)
    ipb_sb = SP.T("ipb_sb", [1, 3 * 512], BF16)
    opT_sb = SP.T("opT_sb", [128, 8 * D], BF16)
    opb_sb = SP.T("opb_sb", [1, D], BF16)
    qkv_sb = SP.T("qkv_sb", [128, L * 3 * D], BF16)
    xnFall = SP.T("xnFall", [128, L * D], BF16)
    oacc_sb = SP.T("oacc_sb", [128, D], F32)
    oF_sb = SP.T("oF_sb", [128, D], BF16)
    scr_sc = SP.T("scr_sc", [128, HM, 8], F32)
    scre_sc = SP.T("scre_sc", [128, HM, 8], F32)
    mxm_sc = SP.T("mxm_sc", [128, HM], F32)
    negmx_sc = SP.T("negmx_sc", [128, HM], F32)
    zm_sc = SP.T("zm_sc", [128, HM], F32)
    zinv_sc = SP.T("zinv_sc", [128, HM], F32)
    zsum_sb = SP.T("zsum_sb", [128, D], F32)
    xhat_sb = SP.T("xhat_sb", [128, D], F32)
    y1bf_sb = SP.T("y1bf_sb", [128, D], BF16)
    logits_sb = SP.T("logits_sb", [128, E], F32)
    mx8_sb = SP.T("mx8_sb", [128, E], F32)
    idxu_sb = SP.T("idxu_sb", [128, E], U32)
    e8_sb = SP.T("e8_sb", [128, E], F32)
    z8_sb = SP.T("z8_sb", [128, 1], F32)
    oh_sb = SP.T("oh_sb", [128, E], F32)

    barrier(block, em, [(sSt, ST_ATTN), (sDv, DV_ATTN), (sPe, PE_ATTN)])

    def _gop(g):
        for k in range(8):
            em.mark(g.dma_start(out=opT_sb[:, k * D:(k + 1) * D],
                                in_=opT[k * 128:(k + 1) * 128, :]), sGop, 16)
        em.mark(g.dma_start(out=opb_sb[:], in_=opb), sGop, 16)
    block.gpsimd(_gop)
    G_OP = em.n(sGop)

    for c4 in range(C4):
        for l in range(L):
            t_i = l * C4 + c4

            def _qtr(t, t_i=t_i, l=l):
                t.wait_ge(sDv, em.n(sDv))
                for k in range(8):
                    i = t.transpose(psT[:, k * 128:(k + 1) * 128],
                                    xn_sb[:, t_i * D + k * 128:t_i * D + (k + 1) * 128],
                                    idb_sb[:])
                em.mark(i, sPe)
            block.tensor(_qtr)

            def _qtrv(v, l=l):
                v.wait_ge(sPe, em.n(sPe))
                em.mark(v.tensor_copy(xnFall[:, l * D:(l + 1) * D], psT[:]), sDv)
            block.vector(_qtrv)

        for half in range(2):
            def _ghalf(g, half=half, c4=c4):
                if not (c4 == 0 and half == 0):
                    g.wait_ge(sPe, em.n(sPe))
                for k in range(8):
                    em.mark(g.dma_start(
                        out=ipT_sb[:, k * 1536:(k + 1) * 1536],
                        in_=ipT[k * 128:(k + 1) * 128,
                                half * 1536:(half + 1) * 1536]), sGip, 16)
                em.mark(g.dma_start(out=ipb_sb[:],
                                    in_=ipb[0:1, half * 1536:(half + 1) * 1536]), sGip, 16)
            block.gpsimd(_ghalf)
            g_half = em.n(sGip)

            for l in range(L):
                for ns in range(3):
                    nseg = half * 3 + ns
                    tgt = psA if ns % 2 == 0 else psB

                    def _qmm(t, l=l, ns=ns, nseg=nseg, tgt=tgt, g_half=g_half):
                        t.wait_ge(sGip, g_half)
                        t.wait_ge(sDv, em.n(sDv))
                        for k in range(8):
                            t.matmul(tgt[:, :], xnFall[:, l * D + k * 128:
                                                       l * D + (k + 1) * 128],
                                     ipT_sb[:, k * 1536 + ns * 512:
                                            k * 1536 + (ns + 1) * 512],
                                     start=(k == 0), stop=False)
                        em.mark(t.matmul(tgt[:, :], onesb_sb[0:1, 0:128],
                                         ipb_sb[0:1, ns * 512:(ns + 1) * 512],
                                         start=False, stop=True), sPe)
                    block.tensor(_qmm)

                    def _qev(v, l=l, nseg=nseg, tgt=tgt):
                        v.wait_ge(sPe, em.n(sPe))
                        em.mark(v.tensor_copy(
                            qkv_sb[:, l * 3 * D + nseg * 512:
                                   l * 3 * D + (nseg + 1) * 512], tgt[:, :]), sDv)
                    block.vector(_qev)

        for l in range(L):
            t_i = l * C4 + c4

            def _sc(v, l=l):
                v.wait_ge(sDv, em.n(sDv))
                scm = []
                for m in range(L):
                    for h in range(HM):
                        i = m * HM + h
                        if i >= 4:
                            v.wait_ge(sDv, scm[i - 4])
                        scm.append(em.mark(v.scalar_tensor_tensor(
                            ln["dummy"][:, (i % 4) * 256:(i % 4 + 1) * 256],
                            qkv_sb[:, l * 3 * D + h * 256:l * 3 * D + (h + 1) * 256],
                            0.0625,
                            qkv_sb[:, m * 3 * D + D + h * 256:
                                   m * 3 * D + D + (h + 1) * 256],
                            op0=ALU.mult, op1=ALU.mult,
                            accum_out=scr_sc[:, h, m:m + 1]), sDv))
                nn = em.n(sDv)
                v.wait_ge(sDv, nn)
                n0 = em.mark(v.reduce_max(mxm_sc[:], scr_sc[:], axis=AX.X), sDv)
                v.wait_ge(sDv, n0)
                em.mark(v.tensor_scalar(negmx_sc[:], mxm_sc[:], -1.0, None,
                                        op0=ALU.mult), sDv)
            block.vector(_sc)

            def _sce(s):
                s.wait_ge(sDv, em.n(sDv))
                for h in range(HM):
                    em.mark(s.activation(scre_sc[:, h, :], scr_sc[:, h, :], AF.Exp,
                                         bias=negmx_sc[:, h:h + 1], scale=1.0), sSe)
            block.scalar(_sce)

            def _so(v, l=l):
                v.wait_ge(sSe, em.n(sSe))
                n0 = em.mark(v.reduce_sum(zm_sc[:], scre_sc[:], axis=AX.X), sDv)
                v.wait_ge(sDv, n0)
                n1 = em.mark(v.reciprocal(zinv_sc[:], zm_sc[:]), sDv)
                v.wait_ge(sDv, n1)
                for h in range(HM):
                    em.mark(v.tensor_scalar(scre_sc[:, h, :], scre_sc[:, h, :],
                                            zinv_sc[:, h:h + 1], None, op0=ALU.mult), sDv)
                nn = em.n(sDv)
                v.wait_ge(sDv, nn)
                marks = {}
                for m in range(L):
                    for h in range(HM):
                        oseg = oacc_sb[:, h * 256:(h + 1) * 256]
                        vseg = qkv_sb[:, m * 3 * D + 2 * D + h * 256:
                                      m * 3 * D + 2 * D + (h + 1) * 256]
                        if m > 0:
                            v.wait_ge(sDv, marks[(m - 1, h)])
                        if m == 0:
                            marks[(m, h)] = em.mark(v.tensor_scalar(
                                oseg, vseg, scre_sc[:, h, m:m + 1], None,
                                op0=ALU.mult), sDv)
                        else:
                            marks[(m, h)] = em.mark(v.scalar_tensor_tensor(
                                oseg, vseg, scre_sc[:, h, m:m + 1], oseg,
                                op0=ALU.mult, op1=ALU.add), sDv)
                nn = em.n(sDv)
                v.wait_ge(sDv, nn)
                em.mark(v.tensor_copy(bf2k[:], oacc_sb[:]), sDv)
            block.vector(_so)

            def _otr(t):
                t.wait_ge(sDv, em.n(sDv))
                for k in range(8):
                    i = t.transpose(psT[:, k * 128:(k + 1) * 128],
                                    bf2k[:, k * 128:(k + 1) * 128], idb_sb[:])
                em.mark(i, sPe)
            block.tensor(_otr)

            def _otrv(v):
                v.wait_ge(sPe, em.n(sPe))
                em.mark(v.tensor_copy(oF_sb[:], psT[:]), sDv)
            block.vector(_otrv)

            def _omm(t):
                t.wait_ge(sGop, G_OP)
                t.wait_ge(sDv, em.n(sDv))
                for k in range(8):
                    t.matmul(psA[:, :], oF_sb[:, k * 128:(k + 1) * 128],
                             opT_sb[:, k * D:k * D + 512], start=(k == 0), stop=False)
                t.matmul(psA[:, :], onesb_sb[0:1, 0:128], opb_sb[0:1, 0:512],
                         start=False, stop=True)
                for k in range(8):
                    t.matmul(psB[:, :], oF_sb[:, k * 128:(k + 1) * 128],
                             opT_sb[:, k * D + 512:(k + 1) * D],
                             start=(k == 0), stop=False)
                em.mark(t.matmul(psB[:, :], onesb_sb[0:1, 0:128], opb_sb[0:1, 512:1024],
                                 start=False, stop=True), sPe)
            block.tensor(_omm)

            def _zs(v, t_i=t_i):
                v.wait_ge(sPe, em.n(sPe))
                em.mark(v.tensor_tensor(zsum_sb[:, 0:512],
                                        xn_sb[:, t_i * D:t_i * D + 512], psA[:, :],
                                        op=ALU.add), sDv)
                em.mark(v.tensor_tensor(zsum_sb[:, 512:1024],
                                        xn_sb[:, t_i * D + 512:(t_i + 1) * D], psB[:, :],
                                        op=ALU.add), sDv)
            block.vector(_zs)

            _ln_chain(block, em, sDv, sSe, zsum_sb[:], xhat_sb[:], ln)

            def _y1(v, t_i=t_i):
                v.wait_ge(sSe, em.n(sSe))
                v.wait_ge(sSt, em.n(sSt))
                n0 = em.mark(v.tensor_tensor(y1bf_sb[:], xhat_sb[:], g1B_sb[:],
                                             op=ALU.mult), sDv)
                v.wait_ge(sDv, n0)
                em.mark(v.tensor_tensor(y1bf_sb[:], y1bf_sb[:], b1B_sb[:],
                                        op=ALU.add), sDv)
            block.vector(_y1)

            def _y1st(sync, t_i=t_i):
                sync.wait_ge(sDv, em.n(sDv))
                em.mark(sync.dma_start(out=o_y1[t_i * 128:(t_i + 1) * 128, :],
                                       in_=y1bf_sb[:]), sSt, 16)
            block.sync(_y1st)

            def _xtr(t):
                t.wait_ge(sSe, em.n(sSe))
                t.wait_ge(sDv, em.n(sDv))
                for k in range(8):
                    i = t.transpose(psF[:, k * 128:(k + 1) * 128],
                                    xhat_sb[:, k * 128:(k + 1) * 128], idf_sb[:])
                em.mark(i, sPe)
            block.tensor(_xtr)

            def _xtrv(v):
                v.wait_ge(sPe, em.n(sPe))
                em.mark(v.tensor_copy(ln["dummy"][:], psF[:]), sDv)
            block.vector(_xtrv)

            def _lg(t):
                t.wait_ge(sDv, em.n(sDv))
                for k in range(8):
                    t.matmul(psF[0:128, 0:E], ln["dummy"][:, k * 128:(k + 1) * 128],
                             wgp_sb[:, k * E:(k + 1) * E],
                             start=(k == 0), stop=False, skip_group_check=True)
                em.mark(t.matmul(psF[0:128, 0:E], onesf_sb[0:1, 0:128], cvec_sb[0:1, :],
                                 start=False, stop=True, skip_group_check=True), sPe)
            block.tensor(_lg)

            def _am(v):
                v.wait_ge(sPe, em.n(sPe))
                n0 = em.mark(v.tensor_copy(logits_sb[:], psF[0:128, 0:E]), sDv)
                v.wait_ge(sDv, n0)
                n1 = em.mark(v.max(mx8_sb[:], logits_sb[:]), sDv)
                v.wait_ge(sDv, n1)
                n2 = em.mark(v.max_index(idxu_sb[:], mx8_sb[:], logits_sb[:]), sDv)
                v.wait_ge(sDv, n2)
                em.mark(v.tensor_scalar(ln["negmu"][:], mx8_sb[:, 0:1], -1.0, None,
                                        op0=ALU.mult), sDv)
            block.vector(_am)

            def _cv(s, t_i=t_i):
                s.wait_ge(sDv, em.n(sDv))
                em.mark(s.activation(idx_sb[:, t_i:t_i + 1], idxu_sb[:, 0:1],
                                     AF.Identity, bias=0.0, scale=1.0), sSe)
                em.mark(s.activation(e8_sb[:], logits_sb[:], AF.Exp,
                                     bias=ln["negmu"][:, 0:1], scale=1.0), sSe)
            block.scalar(_cv)

            def _gt(v, t_i=t_i):
                v.wait_ge(sSe, em.n(sSe))
                n0 = em.mark(v.reduce_sum(z8_sb[:], e8_sb[:], axis=AX.X), sDv)
                v.wait_ge(sDv, n0)
                n1 = em.mark(v.reciprocal(gate_sb[:, t_i:t_i + 1], z8_sb[:]), sDv)
                v.wait_ge(sDv, n1)
                em.mark(v.tensor_scalar(oh_sb[:], iotaE_sb[:], idx_sb[:, t_i:t_i + 1],
                                        0.0, op0=ALU.subtract, op1=ALU.is_equal), sDv)
            block.vector(_gt)

            def _cnt(t, t_i=t_i):
                t.wait_ge(sDv, em.n(sDv))
                em.mark(t.matmul(psC[0:1, t_i * E:(t_i + 1) * E], onescol_sb[:],
                                 oh_sb[:], start=True, stop=True,
                                 skip_group_check=True), sPe)
            block.tensor(_cnt)

    def _cfin(v):
        v.wait_ge(sPe, em.n(sPe))
        for l in range(L):
            v.wait_ge(sDv, em.n(sDv))
            em.mark(v.tensor_copy(cnt_sb[0:1, l * E:(l + 1) * E],
                                  psC[0:1, (l * C4) * E:(l * C4) * E + E]), sDv)
            for c4 in range(1, C4):
                v.wait_ge(sDv, em.n(sDv))
                em.mark(v.tensor_tensor(
                    cnt_sb[0:1, l * E:(l + 1) * E],
                    cnt_sb[0:1, l * E:(l + 1) * E],
                    psC[0:1, (l * C4 + c4) * E:(l * C4 + c4) * E + E],
                    op=ALU.add), sDv)
    block.vector(_cfin)

    def _fin(sync):
        sync.wait_ge(sDv, em.n(sDv))
        em.mark(sync.dma_start(out=o_cnt, in_=cnt_sb[:]), sSt, 16)
        em.mark(sync.dma_start(out=o_idx, in_=idx_sb[:]), sSt, 16)
        em.mark(sync.dma_start(out=o_gate, in_=gate_sb[:]), sSt, 16)
        sync.wait_ge(sSt, em.n(sSt))
    block.sync(_fin)

    SP.close()
    G.close()
    es.close()
    return nc



def build_launch2():
    nc = bass.Bass("TRN2", target_bir_lowering=False)

    x_in = nc.dram_tensor("x", [TOK, D], F32, kind="ExternalInput").ap()
    xn_in = nc.dram_tensor("xn", [TOK, D], BF16, kind="ExternalInput").ap()
    y1_in = nc.dram_tensor("y1", [TOK, D], BF16, kind="ExternalInput").ap()
    idx_in = nc.dram_tensor("idxv", [128, TILES], F32, kind="ExternalInput").ap()
    gate_in = nc.dram_tensor("gatev", [128, TILES], F32, kind="ExternalInput").ap()
    mz_in = nc.dram_tensor("mzall", [CORES, 2 * HA * L], F32, kind="ExternalInput").ap()
    s_in = nc.dram_tensor("sall", [CORES, L * D], F32, kind="ExternalInput").ap()
    cnt_in = nc.dram_tensor("cntall", [CORES * L, E], F32, kind="ExternalInput").ap()
    wcm_in = nc.dram_tensor("wcm", [CORES * L, L], F32, kind="ExternalInput").ap()
    w12_in = nc.dram_tensor("w12all", [E * D, D], BF16, kind="ExternalInput").ap()
    b12_in = nc.dram_tensor("b12all", [E, D], F32, kind="ExternalInput").ap()
    wfT_in = nc.dram_tensor("wfT", [D, D], F32, kind="ExternalInput").ap()
    lng_in = nc.dram_tensor("lng", [1, D], F32, kind="ExternalInput").ap()
    lnb_in = nc.dram_tensor("lnb", [1, D], F32, kind="ExternalInput").ap()
    g2_in = nc.dram_tensor("g2r", [1, D], F32, kind="ExternalInput").ap()
    b2r_in = nc.dram_tensor("b2r", [1, D], F32, kind="ExternalInput").ap()
    idf = nc.dram_tensor("idf", [128, 128], F32, kind="ExternalInput").ap()
    idb = nc.dram_tensor("idb", [128, 128], BF16, kind="ExternalInput").ap()
    onesrow_b = nc.dram_tensor("onesrow_b", [1, 128], BF16, kind="ExternalInput").ap()
    onesrow_f = nc.dram_tensor("onesrow_f", [1, 128], F32, kind="ExternalInput").ap()
    onescol_f = nc.dram_tensor("onescol_f", [128, 1], F32, kind="ExternalInput").ap()
    ltri_in = nc.dram_tensor("ltri", [128, 128], F32, kind="ExternalInput").ap()
    iotaE_in = nc.dram_tensor("iotaE", [128, E], F32, kind="ExternalInput").ap()

    o_out = nc.dram_tensor("o_out", [TOK, D], F32, kind="ExternalOutput").ap()

    staging = nc.dram_tensor("staging", [E * CPC, D], BF16)
    outstage = nc.dram_tensor("outstage", [E * CPC, D], BF16)

    from contextlib import ExitStack
    es = ExitStack()
    P = lambda name, shape, dt: es.enter_context(nc.psum_tensor(name, shape, dt))

    G = Scope(nc)
    xn_sb = G.T("xn_sb", [128, TILES * D], BF16)
    y1_sb = G.T("y1_sb", [128, TILES * D], BF16)
    fhB_sb = G.T("fhB_sb", [128, L * D], BF16)
    lngB = G.T("lngB", [128, D], BF16)
    lnbB = G.T("lnbB", [128, D], BF16)
    g2B = G.T("g2B", [128, D], BF16)
    b2B = G.T("b2B", [128, D], BF16)
    idf_sb = G.T("idf_sb", [128, 128], F32)
    idb_sb = G.T("idb_sb", [128, 128], BF16)
    onesb_sb = G.T("onesb_sb", [1, 128], BF16)
    onesf_sb = G.T("onesf_sb", [1, 128], F32)
    onescol_sb = G.T("onescol_sb", [128, 1], F32)
    ltri_sb = G.T("ltri_sb", [128, 128], F32)
    iotaE_sb = G.T("iotaE_sb", [128, E], F32)
    idx_sb = G.T("idx_sb", [128, TILES], F32)
    gate_sb = G.T("gate_sb", [128, TILES], F32)
    slotu_sb = G.T("slotu_sb", [128, TILES], U32)
    cs_sb = G.T("cs_sb", [1, TILES * E], F32)
    runl_sb = G.T("runl_sb", [1, TILES * E], F32)
    runa_sb = G.T("runa_sb", [1, TILES * E], F32)
    zero_sb = G.T("zero_sb", [128, D], BF16)
    ln = {k: G.T("ln_" + k, [128, 1], F32) for k in
          ("sx", "negmu", "svar", "vep", "rstd", "nmr")}
    ln["dummy"] = G.T("ln_dummy", [128, D], F32)
    smalls = {k: G.T("sm_" + k, [128, 1], F32) for k in
              ("loc", "lpos", "keep", "gv", "slot", "ovf")}
    oh_sb = G.T("oh_sb", [128, E], F32)
    ohrun_sb = G.T("ohrun_sb", [128, E], F32)

    psA = P("psA", [128, 512], F32)
    psB = P("psB", [128, 512], F32)
    psT = P("psT", [128, D], BF16)
    psF = P("psF", [128, D], F32)
    psS = P("psS", [1, D], F32)
    psC = P("psC", [L, 512], F32)

    sLd = es.enter_context(nc.semaphore("sLd"))
    sG = es.enter_context(nc.semaphore("sG"))
    sGz = es.enter_context(nc.semaphore("sGz"))
    sGsc = es.enter_context(nc.semaphore("sGsc"))
    sGb = es.enter_context(nc.semaphore("sGb"))
    sGg = es.enter_context(nc.semaphore("sGg"))
    sLdsl = es.enter_context(nc.semaphore("sLdsl"))
    sLdfh = es.enter_context(nc.semaphore("sLdfh"))
    sLdb = es.enter_context(nc.semaphore("sLdb"))
    sLdw = es.enter_context(nc.semaphore("sLdw"))
    sLdst = es.enter_context(nc.semaphore("sLdst"))
    sLdX = [es.enter_context(nc.semaphore(f"sLdX{i}")) for i in range(2)]
    sPe = es.enter_context(nc.semaphore("sPe"))
    sDv = es.enter_context(nc.semaphore("sDv"))
    sSe = es.enter_context(nc.semaphore("sSe"))
    sSt = es.enter_context(nc.semaphore("sSt"))
    block = es.enter_context(nc.Block())
    em = Emit()

    SA = Scope(nc)
    wbig_sb = SA.T("wbig_sb", [128, 8 * D], BF16)
    mz_sb = SA.T("mz_sb", [CORES, 2 * HA * L], F32)
    mzT_sb = SA.T("mzT_sb", [HA * L, CORES], F32)
    zT0_sb = SA.T("zT0_sb", [HA * L, CORES], F32)
    mg_sb = SA.T("mg_sb", [HA * L, 1], F32)
    negmg_sb = SA.T("negmg_sb", [HA * L, 1], F32)
    wT_sb = SA.T("wT_sb", [HA * L, CORES], F32)
    zt_sb = SA.T("zt_sb", [HA * L, CORES], F32)
    zg_sb = SA.T("zg_sb", [HA * L, 1], F32)
    zgi_sb = SA.T("zgi_sb", [HA * L, 1], F32)
    w8_sb = SA.T("w8_sb", [CORES, HA * L], F32)
    s_l = SA.T("s_l", [CORES, D], F32)
    ctx_l = SA.T("ctx_l", [1, D], F32)
    ctxF_sb = SA.T("ctxF_sb", [128, 8 * L], BF16)
    fh_sb = SA.T("fh_sb", [L, D], F32)
    fhrow_sb = SA.T("fhrow_sb", [1, D], F32)
    cnt2_sb = SA.T("cnt2_sb", [CORES * L, E], F32)
    wcm_sb = SA.T("wcm_sb", [CORES * L, L], F32)
    base_sb = SA.T("base_sb", [L, E], F32)
    baser_sb = SA.T("baser_sb", [1, L * E], F32)
    stage_sb = SA.T("stage_sb", [128, D], BF16)
    stF_sb = SA.T("stF_sb", [128, D], BF16)
    eo_sb = SA.T("eo_sb", [128, D], BF16)
    b12e_sb = SA.T("b12e_sb", [1, D], BF16)

    def _ld(sync):
        for t_i in range(TILES):
            em.mark(sync.dma_start(out=xn_sb[:, t_i * D:(t_i + 1) * D],
                                   in_=xn_in[t_i * 128:(t_i + 1) * 128, :]), sLd, 16)
            em.mark(sync.dma_start(out=y1_sb[:, t_i * D:(t_i + 1) * D],
                                   in_=y1_in[t_i * 128:(t_i + 1) * 128, :]), sLd, 16)
        for dst, src in ((idf_sb[:], idf), (idb_sb[:], idb), (onesb_sb[:], onesrow_b),
                         (onesf_sb[:], onesrow_f), (onescol_sb[:], onescol_f),
                         (ltri_sb[:], ltri_in), (iotaE_sb[:], iotaE_in),
                         (idx_sb[:], idx_in), (gate_sb[:], gate_in),
                         (mz_sb[:], mz_in), (wcm_sb[:], wcm_in), (cnt2_sb[:], cnt_in)):
            em.mark(sync.dma_start(out=dst, in_=src), sLd, 16)
    block.sync(_ld)
    LD0 = em.n(sLd)

    def _gld(g):
        for k in range(8):
            em.mark(g.dma_start(out=wbig_sb[:, k * D:(k + 1) * D],
                                in_=wfT_in[k * 128:(k + 1) * 128, :]), sG, 16)
        for dst, src in ((lngB, lng_in), (lnbB, lnb_in), (g2B, g2_in), (b2B, b2r_in)):
            em.mark(g.dma_start(out=dst[0:1, :], in_=src), sG, 16)
    block.gpsimd(_gld)
    G0 = em.n(sG)

    def _z(v):
        em.mark(v.memset(zero_sb[:], 0.0), sDv)
    block.vector(_z)

    bc_cell = []

    def _zs(g):
        bc_cell.append(g.to_reg(E * CPC - 1))
        g.wait_ge(sDv, em.n(sDv))
        for r in range(E * CPC // 128):
            em.mark(g.dma_start(out=staging.ap()[r * 128:(r + 1) * 128, :],
                                in_=zero_sb[:]), sGz, 16)
    block.gpsimd(_zs)
    G_ZERO = em.n(sGz)

    for dst in (lngB, lnbB, g2B, b2B):
        def _bct(t, dst=dst):
            t.wait_ge(sG, G0)
            t.wait_ge(sLd, LD0)
            t.wait_ge(sDv, em.n(sDv))
            t.matmul(psF[:, 0:512], onesb_sb[0:1, :], dst[0:1, 0:512],
                     start=True, stop=True)
            em.mark(t.matmul(psF[:, 512:1024], onesb_sb[0:1, :], dst[0:1, 512:1024],
                             start=True, stop=True), sPe)
        block.tensor(_bct)

        def _bcv(v, dst=dst):
            v.wait_ge(sPe, em.n(sPe))
            em.mark(v.tensor_copy(dst[:], psF[:]), sDv)
        block.vector(_bcv)

    def _t1(t):
        t.wait_ge(sDv, em.n(sDv))
        em.mark(t.transpose(psF[0:128, 0:CORES], mz_sb[:], idf_sb[0:CORES, 0:CORES]), sPe)
    block.tensor(_t1)

    def _v1(v):
        v.wait_ge(sPe, em.n(sPe))
        em.mark(v.tensor_copy(mzT_sb[:], psF[0:HA * L, 0:CORES]), sDv)
        n0 = em.mark(v.tensor_copy(zT0_sb[:], psF[HA * L:2 * HA * L, 0:CORES]), sDv)
        v.wait_ge(sDv, n0)
        n1 = em.mark(v.reduce_max(mg_sb[:], mzT_sb[0:HA * L, :], axis=AX.X), sDv)
        v.wait_ge(sDv, n1)
        em.mark(v.tensor_scalar(negmg_sb[:], mg_sb[:], -1.0, None, op0=ALU.mult), sDv)
    block.vector(_v1)

    def _s1(s):
        s.wait_ge(sDv, em.n(sDv))
        em.mark(s.activation(wT_sb[:], mzT_sb[0:HA * L, :], AF.Exp,
                             bias=negmg_sb[:, 0:1], scale=1.0), sSe)
    block.scalar(_s1)

    def _v2(v):
        v.wait_ge(sSe, em.n(sSe))
        n0 = em.mark(v.tensor_tensor(zt_sb[:], wT_sb[:], zT0_sb[:], op=ALU.mult), sDv)
        v.wait_ge(sDv, n0)
        n1 = em.mark(v.reduce_sum(zg_sb[:], zt_sb[:], axis=AX.X), sDv)
        v.wait_ge(sDv, n1)
        n2 = em.mark(v.reciprocal(zgi_sb[:], zg_sb[:]), sDv)
        v.wait_ge(sDv, n2)
        em.mark(v.tensor_scalar(wT_sb[:], wT_sb[:], zgi_sb[:, 0:1], None,
                                op0=ALU.mult), sDv)
    block.vector(_v2)

    def _t2(t):
        t.wait_ge(sDv, em.n(sDv))
        em.mark(t.transpose(psF[0:CORES, 0:HA * L], wT_sb[:],
                            idf_sb[0:HA * L, 0:HA * L]), sPe)
    block.tensor(_t2)

    def _v3(v):
        v.wait_ge(sPe, em.n(sPe))
        em.mark(v.tensor_copy(w8_sb[:], psF[0:CORES, 0:HA * L]), sDv)
    block.vector(_v3)

    for l in range(L):
        def _sll(sync, l=l):
            if l >= 1:
                sync.wait_ge(sPe, em.n(sPe))
            em.mark(sync.dma_start(out=s_l[:], in_=s_in[:, l * D:(l + 1) * D]), sLdsl, 16)
        block.sync(_sll)
        ld_sl = em.n(sLdsl)

        def _ctx(t, l=l, ld_sl=ld_sl):
            t.wait_ge(sLdsl, ld_sl)
            t.wait_ge(sDv, em.n(sDv))
            for h in range(HA):
                i = t.matmul(psS[0:1, h * 128:(h + 1) * 128],
                             w8_sb[:, l * HA + h:l * HA + h + 1],
                             s_l[:, h * 128:(h + 1) * 128],
                             start=True, stop=True, skip_group_check=True)
            em.mark(i, sPe)
        block.tensor(_ctx)

        def _ctxv(v, l=l):
            v.wait_ge(sPe, em.n(sPe))
            em.mark(v.tensor_copy(ctx_l[0:1, :], psS[0:1, :]), sDv)
        block.vector(_ctxv)

        for k in range(8):
            def _cf(t, l=l, k=k):
                t.wait_ge(sDv, em.n(sDv))
                em.mark(t.transpose(psF[:, 0:1],
                                    ctx_l[0:1, k * 128:(k + 1) * 128],
                                    idf_sb[0:1, 0:1]), sPe)
            block.tensor(_cf)

            def _cfv(v, l=l, k=k):
                v.wait_ge(sPe, em.n(sPe))
                em.mark(v.tensor_copy(ctxF_sb[:, k * L + l:k * L + l + 1],
                                      psF[:, 0:1]), sDv)
            block.vector(_cfv)

    def _fh(t):
        t.wait_ge(sG, G0)
        t.wait_ge(sDv, em.n(sDv))
        for k in range(8):
            t.matmul(psA[0:L, :], ctxF_sb[:, k * L:(k + 1) * L],
                     wbig_sb[:, k * D:k * D + 512], start=(k == 0), stop=(k == 7))
        for k in range(8):
            i = t.matmul(psB[0:L, :], ctxF_sb[:, k * L:(k + 1) * L],
                         wbig_sb[:, k * D + 512:(k + 1) * D],
                         start=(k == 0), stop=(k == 7))
        em.mark(i, sPe)
    block.tensor(_fh)
    PE_FH = em.n(sPe)

    def _fhv(v):
        v.wait_ge(sPe, PE_FH)
        em.mark(v.tensor_copy(fh_sb[:, 0:512], psA[0:L, :]), sDv)
        em.mark(v.tensor_copy(fh_sb[:, 512:1024], psB[0:L, :]), sDv)
    block.vector(_fhv)
    DV_FH = em.n(sDv)

    for l in range(L):
        def _fhr(sync, l=l):
            sync.wait_ge(sDv, DV_FH)
            if l >= 1:
                sync.wait_ge(sPe, em.n(sPe))
            em.mark(sync.dma_start(out=fhrow_sb[0:1, :], in_=fh_sb[l:l + 1, :]), sLdfh, 16)
        block.sync(_fhr)
        ld_f = em.n(sLdfh)

        def _fb(t, l=l, ld_f=ld_f):
            t.wait_ge(sLdfh, ld_f)
            t.wait_ge(sDv, em.n(sDv))
            t.matmul(psA[:, :], onesf_sb[0:1, :], fhrow_sb[0:1, 0:512],
                     start=True, stop=True)
            em.mark(t.matmul(psB[:, :], onesf_sb[0:1, :], fhrow_sb[0:1, 512:1024],
                             start=True, stop=True), sPe)
        block.tensor(_fb)

        def _fbv(v, l=l):
            v.wait_ge(sPe, em.n(sPe))
            em.mark(v.tensor_copy(fhB_sb[:, l * D:l * D + 512], psA[:]), sDv)
            em.mark(v.tensor_copy(fhB_sb[:, l * D + 512:(l + 1) * D], psB[:]), sDv)
        block.vector(_fbv)

    def _base(t):
        t.wait_ge(sDv, em.n(sDv))
        em.mark(t.matmul(psC[0:L, 0:E], wcm_sb[:], cnt2_sb[:],
                         start=True, stop=True), sPe)
    block.tensor(_base)

    def _basev(v):
        v.wait_ge(sPe, em.n(sPe))
        em.mark(v.tensor_copy(base_sb[:], psC[0:L, 0:E]), sDv)
    block.vector(_basev)

    def _baser(sync):
        sync.wait_ge(sDv, em.n(sDv))
        em.mark(sync.dma_start(out=baser_sb[:], in_=base_sb[:]), sLdb, 16)
    block.sync(_baser)
    LD_BASE = em.n(sLdb)

    for t_i in range(TILES):
        def _oh(v, t_i=t_i):
            v.wait_ge(sPe, em.n(sPe))
            v.wait_ge(sDv, em.n(sDv))
            em.mark(v.tensor_scalar(oh_sb[:], iotaE_sb[:], idx_sb[:, t_i:t_i + 1],
                                    0.0, op0=ALU.subtract, op1=ALU.is_equal), sDv)
        block.vector(_oh)

        def _cs(t, t_i=t_i):
            t.wait_ge(sDv, em.n(sDv))
            em.mark(t.matmul(psC[0:1, t_i * E:(t_i + 1) * E], onescol_sb[:], oh_sb[:],
                             start=True, stop=True, skip_group_check=True), sPe)
        block.tensor(_cs)

    def _csv(v):
        v.wait_ge(sPe, em.n(sPe))
        em.mark(v.tensor_copy(cs_sb[0:1, :], psC[0:1, 0:TILES * E]), sDv)
    block.vector(_csv)

    def _pre(v):
        v.wait_ge(sDv, em.n(sDv))
        v.wait_ge(sLdb, LD_BASE)
        for t_i in range(TILES):
            l, c4 = divmod(t_i, C4)
            if c4 == 0:
                em.mark(v.tensor_copy(runl_sb[0:1, t_i * E:(t_i + 1) * E],
                                      baser_sb[0:1, l * E:(l + 1) * E]), sDv)
            else:
                v.wait_ge(sDv, em.n(sDv))
                em.mark(v.tensor_tensor(runl_sb[0:1, t_i * E:(t_i + 1) * E],
                                        runl_sb[0:1, (t_i - 1) * E:t_i * E],
                                        cs_sb[0:1, (t_i - 1) * E:t_i * E],
                                        op=ALU.add), sDv)
            if t_i == 0:
                em.mark(v.memset(runa_sb[0:1, 0:E], 0.0), sDv)
            else:
                v.wait_ge(sDv, em.n(sDv))
                em.mark(v.tensor_tensor(runa_sb[0:1, t_i * E:(t_i + 1) * E],
                                        runa_sb[0:1, (t_i - 1) * E:t_i * E],
                                        cs_sb[0:1, (t_i - 1) * E:t_i * E],
                                        op=ALU.add), sDv)
    block.vector(_pre)

    for t_i in range(TILES):
        def _oh2(v, t_i=t_i):
            v.wait_ge(sDv, em.n(sDv))
            v.wait_ge(sPe, em.n(sPe))
            em.mark(v.tensor_scalar(oh_sb[:], iotaE_sb[:], idx_sb[:, t_i:t_i + 1],
                                    0.0, op0=ALU.subtract, op1=ALU.is_equal), sDv)
        block.vector(_oh2)

        def _lt(t, t_i=t_i):
            t.wait_ge(sDv, em.n(sDv))
            t.matmul(psF[0:128, 0:E], ltri_sb[:], oh_sb[:],
                     start=True, stop=False, skip_group_check=True)
            t.matmul(psF[0:128, 0:E], onesf_sb[0:1, 0:128],
                     runl_sb[0:1, t_i * E:(t_i + 1) * E],
                     start=False, stop=True, skip_group_check=True)
            t.matmul(psF[0:128, E:2 * E], ltri_sb[:], oh_sb[:],
                     start=True, stop=False, skip_group_check=True)
            em.mark(t.matmul(psF[0:128, E:2 * E], onesf_sb[0:1, 0:128],
                             runa_sb[0:1, t_i * E:(t_i + 1) * E],
                             start=False, stop=True, skip_group_check=True), sPe)
        block.tensor(_lt)

        def _sel(v, t_i=t_i):
            v.wait_ge(sPe, em.n(sPe))
            n0 = em.mark(v.tensor_tensor(ohrun_sb[:], psF[0:128, 0:E], oh_sb[:],
                                         op=ALU.mult), sDv)
            v.wait_ge(sDv, n0)
            n1 = em.mark(v.reduce_sum(smalls["loc"][:], ohrun_sb[:], axis=AX.X), sDv)
            v.wait_ge(sDv, n1)
            n2 = em.mark(v.tensor_tensor(ohrun_sb[:], psF[0:128, E:2 * E], oh_sb[:],
                                         op=ALU.mult), sDv)
            v.wait_ge(sDv, n2)
            n3 = em.mark(v.reduce_sum(smalls["lpos"][:], ohrun_sb[:], axis=AX.X), sDv)
            v.wait_ge(sDv, n3)
            n4 = em.mark(v.tensor_scalar(smalls["keep"][:], smalls["loc"][:], float(CAP),
                                         None, op0=ALU.is_lt), sDv)
            v.wait_ge(sDv, n4)
            em.mark(v.tensor_tensor(smalls["gv"][:], smalls["keep"][:],
                                    gate_sb[:, t_i:t_i + 1], op=ALU.mult), sDv)
            n6 = em.mark(v.tensor_scalar(smalls["slot"][:], idx_sb[:, t_i:t_i + 1],
                                         float(CPC), smalls["lpos"][:, 0:1],
                                         op0=ALU.mult, op1=ALU.add), sDv)
            n7 = em.mark(v.tensor_scalar(smalls["ovf"][:], smalls["lpos"][:], float(CPC),
                                         None, op0=ALU.is_ge), sDv)
            v.wait_ge(sDv, n7)
            n8 = em.mark(v.scalar_tensor_tensor(smalls["slot"][:], smalls["ovf"][:], OOB,
                                                smalls["slot"][:], op0=ALU.mult,
                                                op1=ALU.add), sDv)
            v.wait_ge(sDv, n8)
            em.mark(v.tensor_copy(gate_sb[:, t_i:t_i + 1], smalls["gv"][:]), sDv)
        block.vector(_sel)

        def _cvt(s, t_i=t_i):
            s.wait_ge(sDv, em.n(sDv))
            em.mark(s.activation(slotu_sb[:, t_i:t_i + 1], smalls["slot"][:], AF.Identity,
                                 bias=0.0, scale=1.0), sSe)
        block.scalar(_cvt)

        def _scat(g, t_i=t_i):
            g.wait_ge(sSe, em.n(sSe))
            g.wait_ge(sGz, G_ZERO)
            em.mark(g.indirect_dma_start(
                out=staging.ap()[0:128, :],
                out_offset=IndirectOffsetOnAxis(slotu_sb[:, t_i:t_i + 1], axis=0),
                in_=y1_sb[:, t_i * D:(t_i + 1) * D],
                in_offset=None,
                bounds_check=bc_cell[0],
                oob_is_err=False), sGsc, 16)
        block.gpsimd(_scat)

    G_SCAT = em.n(sGsc)

    for e in range(E):
        def _wld(sync, e=e):
            if e == 0:
                sync.wait_ge(sGsc, G_SCAT)
                sync.wait_ge(sPe, PE_FH)
            else:
                sync.wait_ge(sPe, em.n(sPe))
            for k in range(8):
                em.mark(sync.dma_start(
                    out=wbig_sb[:, k * D:(k + 1) * D],
                    in_=w12_in[e * D + k * 128:e * D + (k + 1) * 128, :]), sLdw, 16)
        block.sync(_wld)
        ld_w = em.n(sLdw)

        def _bld(g, e=e):
            if e == 0:
                g.wait_ge(sGsc, G_SCAT)
            else:
                g.wait_ge(sPe, em.n(sPe))
            em.mark(g.dma_start(out=b12e_sb[0:1, :], in_=b12_in[e:e + 1, :]), sGb, 16)
        block.gpsimd(_bld)
        g_b = em.n(sGb)

        for r in range(CPC // 128):
            def _sld(sync, e=e, r=r):
                sync.wait_ge(sPe, em.n(sPe))
                em.mark(sync.dma_start(
                    out=stage_sb[:],
                    in_=staging.ap()[e * CPC + r * 128:e * CPC + (r + 1) * 128, :]),
                    sLdst, 16)
            block.sync(_sld)
            ld_s = em.n(sLdst)

            def _str(t, ld_s=ld_s):
                t.wait_ge(sLdst, ld_s)
                t.wait_ge(sDv, em.n(sDv))
                for k in range(8):
                    i = t.transpose(psT[:, k * 128:(k + 1) * 128],
                                    stage_sb[:, k * 128:(k + 1) * 128], idb_sb[:])
                em.mark(i, sPe)
            block.tensor(_str)

            def _stv(v):
                v.wait_ge(sPe, em.n(sPe))
                em.mark(v.tensor_copy(stF_sb[:], psT[:]), sDv)
            block.vector(_stv)

            def _emm(t, e=e, ld_w=ld_w, g_b=g_b):
                t.wait_ge(sLdw, ld_w)
                t.wait_ge(sGb, g_b)
                t.wait_ge(sDv, em.n(sDv))
                for k in range(8):
                    t.matmul(psA[:, :], stF_sb[:, k * 128:(k + 1) * 128],
                             wbig_sb[:, k * D:k * D + 512], start=(k == 0), stop=False)
                t.matmul(psA[:, :], onesb_sb[0:1, 0:128], b12e_sb[0:1, 0:512],
                         start=False, stop=True)
                for k in range(8):
                    t.matmul(psB[:, :], stF_sb[:, k * 128:(k + 1) * 128],
                             wbig_sb[:, k * D + 512:(k + 1) * D],
                             start=(k == 0), stop=False)
                em.mark(t.matmul(psB[:, :], onesb_sb[0:1, 0:128], b12e_sb[0:1, 512:1024],
                                 start=False, stop=True), sPe)
            block.tensor(_emm)

            def _eev(v):
                v.wait_ge(sPe, em.n(sPe))
                v.wait_ge(sSt, em.n(sSt))
                em.mark(v.tensor_copy(eo_sb[:, 0:512], psA[:]), sDv)
                em.mark(v.tensor_copy(eo_sb[:, 512:1024], psB[:]), sDv)
            block.vector(_eev)

            def _est(sync, e=e, r=r):
                sync.wait_ge(sDv, em.n(sDv))
                em.mark(sync.dma_start(
                    out=outstage.ap()[e * CPC + r * 128:e * CPC + (r + 1) * 128, :],
                    in_=eo_sb[:]), sSt, 16)
            block.sync(_est)

    ST_EXP = em.n(sSt)
    DV_EXP = em.n(sDv)
    SA.close()

    SC = Scope(nc)
    xbuf = [SC.T(f"xbuf{i}", [128, D], F32) for i in range(2)]
    gz_sb = SC.T("gz_sb", [128, D], BF16)
    moe_sb = SC.T("moe_sb", [128, D], F32)
    z2_sb = SC.T("z2_sb", [128, D], F32)
    xh2_sb = SC.T("xh2_sb", [128, D], F32)
    y2_sb = SC.T("y2_sb", [128, D], F32)
    aout_sb = SC.T("aout_sb", [128, D], F32)
    out_sb = SC.T("out_sb", [128, D], F32)

    barrier(block, em, [(sSt, ST_EXP), (sDv, DV_EXP)])

    for t_i in range(TILES):
        l = t_i // C4
        buf = t_i % 2

        def _xl(sync, t_i=t_i, buf=buf):
            if t_i >= 2:
                sync.wait_ge(sDv, em.n(sDv))
            em.mark(sync.dma_start(out=xbuf[buf][:],
                                   in_=x_in[t_i * 128:(t_i + 1) * 128, :]), sLdX[buf], 16)
        block.sync(_xl)
        ldx = em.n(sLdX[buf])

        def _gzz(v, t_i=t_i):
            v.wait_ge(sGg, em.n(sGg))
            v.wait_ge(sDv, em.n(sDv))
            em.mark(v.memset(gz_sb[:], 0.0), sDv)
        block.vector(_gzz)

        def _gth(g, t_i=t_i):
            g.wait_ge(sSt, ST_EXP)
            g.wait_ge(sDv, em.n(sDv))
            em.mark(g.indirect_dma_start(
                out=gz_sb[:],
                out_offset=None,
                in_=outstage.ap()[0:128, :],
                in_offset=IndirectOffsetOnAxis(slotu_sb[:, t_i:t_i + 1], axis=0),
                bounds_check=bc_cell[0],
                oob_is_err=False), sGg, 16)
        block.gpsimd(_gth)
        gn = em.n(sGg)

        def _moe(v, t_i=t_i, gn=gn):
            v.wait_ge(sGg, gn)
            n0 = em.mark(v.tensor_scalar(moe_sb[:], gz_sb[:], gate_sb[:, t_i:t_i + 1],
                                         None, op0=ALU.mult), sDv)
            v.wait_ge(sDv, n0)
            em.mark(v.tensor_tensor(z2_sb[:], y1_sb[:, t_i * D:(t_i + 1) * D],
                                    moe_sb[:], op=ALU.add), sDv)
        block.vector(_moe)

        _ln_chain(block, em, sDv, sSe, z2_sb[:], xh2_sb[:], ln)

        def _y2(v):
            v.wait_ge(sSe, em.n(sSe))
            n0 = em.mark(v.tensor_tensor(y2_sb[:], xh2_sb[:], g2B[:], op=ALU.mult), sDv)
            v.wait_ge(sDv, n0)
            em.mark(v.tensor_tensor(y2_sb[:], y2_sb[:], b2B[:], op=ALU.add), sDv)
        block.vector(_y2)

        def _ya(v, t_i=t_i, l=l):
            v.wait_ge(sDv, em.n(sDv))
            em.mark(v.tensor_tensor(z2_sb[:], xn_sb[:, t_i * D:(t_i + 1) * D],
                                    fhB_sb[:, l * D:(l + 1) * D], op=ALU.add), sDv)
        block.vector(_ya)

        _ln_chain(block, em, sDv, sSe, z2_sb[:], xh2_sb[:], ln)

        def _ao(v):
            v.wait_ge(sSe, em.n(sSe))
            n0 = em.mark(v.tensor_tensor(aout_sb[:], xh2_sb[:], lngB[:], op=ALU.mult), sDv)
            v.wait_ge(sDv, n0)
            em.mark(v.tensor_tensor(aout_sb[:], aout_sb[:], lnbB[:], op=ALU.add), sDv)
        block.vector(_ao)

        def _sum(v, buf=buf, ldx=ldx):
            v.wait_ge(sLdX[buf], ldx)
            v.wait_ge(sDv, em.n(sDv))
            v.wait_ge(sSt, em.n(sSt))
            n0 = em.mark(v.tensor_tensor(out_sb[:], xbuf[buf][:], aout_sb[:],
                                         op=ALU.add), sDv)
            v.wait_ge(sDv, n0)
            em.mark(v.tensor_tensor(out_sb[:], out_sb[:], y2_sb[:], op=ALU.add), sDv)
        block.vector(_sum)

        def _ost(sync, t_i=t_i):
            sync.wait_ge(sDv, em.n(sDv))
            em.mark(sync.dma_start(out=o_out[t_i * 128:(t_i + 1) * 128, :],
                                   in_=out_sb[:]), sSt, 16)
        block.sync(_ost)

    def _end(sync):
        sync.wait_ge(sSt, em.n(sSt))
    block.sync(_end)

    SC.close()
    G.close()
    es.close()
    return nc



def _consts():
    idf = np.eye(128, dtype=np.float32)
    idb = np.eye(128, dtype=BF16NP)
    onesrow_b = np.ones((1, 128), dtype=BF16NP)
    onesrow_f = np.ones((1, 128), dtype=np.float32)
    onescol_f = np.ones((128, 1), dtype=np.float32)
    iotaE = np.tile(np.arange(E, dtype=np.float32), (128, 1))
    ltri = np.triu(np.ones((128, 128), dtype=np.float32), 1)
    return idf, idb, onesrow_b, onesrow_f, onescol_f, iotaE, ltri


def kernel(L0_attn_h, L0_attn_Wl, L0_attn_Wr, L0_attn_att_l, L0_attn_att_r,
           L0_attn_Wf, L0_attn_ln_gamma, L0_attn_ln_beta, L0_ffn_in_proj_w,
           L0_ffn_in_proj_b, L0_ffn_out_proj_w, L0_ffn_out_proj_b, L0_ffn_ln1_g,
           L0_ffn_ln1_b, L0_ffn_ln2_g, L0_ffn_ln2_b, L0_ffn_wg, L0_ffn_w1,
           L0_ffn_b1, L0_ffn_w2, L0_ffn_b2):
    f32 = np.float32
    x = np.asarray(L0_attn_h, f32)
    idf, idb, onesrow_b, onesrow_f, onescol_f, iotaE, ltri = _consts()

    wrT = np.ascontiguousarray(np.asarray(L0_attn_Wr, f32).T)
    attr = np.ascontiguousarray(np.asarray(L0_attn_att_r, f32).reshape(1, 128))
    ipT = np.ascontiguousarray(np.asarray(L0_ffn_in_proj_w, f32).T)
    ipb = np.asarray(L0_ffn_in_proj_b, f32).reshape(1, 3 * D)
    opT = np.ascontiguousarray(np.asarray(L0_ffn_out_proj_w, f32).T)
    opb = np.asarray(L0_ffn_out_proj_b, f32).reshape(1, D)
    g1 = np.asarray(L0_ffn_ln1_g, f32).reshape(1, D)
    b1r = np.asarray(L0_ffn_ln1_b, f32).reshape(1, D)
    wgA = np.ascontiguousarray(np.asarray(L0_ffn_wg, f32))
    w1 = np.asarray(L0_ffn_w1, f32)
    b1 = np.asarray(L0_ffn_b1, f32)
    w2 = np.asarray(L0_ffn_w2, f32)
    b2 = np.asarray(L0_ffn_b2, f32)
    wfT = np.ascontiguousarray(np.asarray(L0_attn_Wf, f32).T)
    lng = np.asarray(L0_attn_ln_gamma, f32).reshape(1, D)
    lnb = np.asarray(L0_attn_ln_beta, f32).reshape(1, D)
    g2r = np.asarray(L0_ffn_ln2_g, f32).reshape(1, D)
    b2r = np.asarray(L0_ffn_ln2_b, f32).reshape(1, D)

    xs = [np.ascontiguousarray(x[:, c * NLOC:(c + 1) * NLOC, :].reshape(TOK, D))
          for c in range(CORES)]

    nc1 = build_launch1()
    in_maps1 = []
    for c in range(CORES):
        in_maps1.append({
            "x": xs[c], "wrT": wrT, "attr": attr, "ipT": ipT, "ipb": ipb,
            "opT": opT, "opb": opb, "g1": g1, "b1r": b1r, "wg": wgA,
            "w1T": np.ascontiguousarray(w1[c].T),
            "w2e": np.ascontiguousarray(w2[c]),
            "b1e": np.ascontiguousarray(b1[c].reshape(FF // 128, 128).T),
            "b2e": b2[c].reshape(1, D),
            "idf": idf, "idb": idb, "onesrow_b": onesrow_b,
            "onesrow_f": onesrow_f, "onescol_f": onescol_f, "iotaE": iotaE,
        })
    res1 = run_bass_kernel_spmd(nc1, in_maps1, core_ids=list(range(CORES)))
    r1 = res1.results

    mz_all = np.stack([
        np.concatenate([np.asarray(r1[c]["o_m"], f32).T.reshape(-1),
                        np.asarray(r1[c]["o_z"], f32).T.reshape(-1)])
        for c in range(CORES)], axis=0).astype(f32)
    s_all = np.ascontiguousarray(
        np.stack([np.asarray(r1[c]["o_s"], f32).reshape(-1) for c in range(CORES)]))
    cnt_all = np.ascontiguousarray(
        np.stack([np.asarray(r1[c]["o_cnt"], f32).reshape(L, E)
                  for c in range(CORES)]).reshape(CORES * L, E))
    w12_all = np.ascontiguousarray(
        np.concatenate([np.asarray(r1[c]["o_w12"]) for c in range(CORES)], axis=0))
    b12_all = np.ascontiguousarray(
        np.stack([np.asarray(r1[c]["o_b12"], f32).reshape(-1) for c in range(CORES)]))

    nc2 = build_launch2()
    in_maps2 = []
    for c in range(CORES):
        wcm = np.zeros((CORES * L, L), dtype=f32)
        for cp in range(CORES):
            for lp in range(L):
                for l in range(L):
                    if lp < l or (lp == l and cp < c):
                        wcm[cp * L + lp, l] = 1.0
        in_maps2.append({
            "x": xs[c],
            "xn": np.asarray(r1[c]["o_xn"]),
            "y1": np.asarray(r1[c]["o_y1"]),
            "idxv": np.asarray(r1[c]["o_idx"], f32),
            "gatev": np.asarray(r1[c]["o_gate"], f32),
            "mzall": mz_all, "sall": s_all, "cntall": cnt_all, "wcm": wcm,
            "w12all": w12_all, "b12all": b12_all,
            "wfT": wfT, "lng": lng, "lnb": lnb, "g2r": g2r, "b2r": b2r,
            "idf": idf, "idb": idb, "onesrow_b": onesrow_b,
            "onesrow_f": onesrow_f, "onescol_f": onescol_f,
            "ltri": ltri, "iotaE": iotaE,
        })
    res2 = run_bass_kernel_spmd(nc2, in_maps2, core_ids=list(range(CORES)))

    out = np.empty((L, NSEQ, D), dtype=f32)
    for c in range(CORES):
        oc = np.asarray(res2.results[c]["o_out"], f32).reshape(L, NLOC, D)
        out[:, c * NLOC:(c + 1) * NLOC, :] = oc
    return out
